# version 01
# baseline (speedup 1.0000x reference)
import numpy as np
import ml_dtypes
import concourse.bass as bass
import concourse.mybir as mybir
from concourse.ap import AP
from concourse.bass_utils import run_bass_kernel_spmd

F32 = mybir.dt.float32
BF16 = mybir.dt.bfloat16
I32 = mybir.dt.int32
AF = mybir.ActivationFunctionType
ALU = mybir.AluOpType
AX = mybir.AxisListType

T = 4096
D = 1024
NDC = D // 128

BF = ml_dtypes.bfloat16



class Tok:
    __slots__ = ("eng", "idx", "marked", "semval", "dma")

    def __init__(self, eng, idx, dma):
        self.eng = eng
        self.idx = idx
        self.marked = False
        self.semval = None
        self.dma = dma


class Emitter:
    ENGS = ("t", "v", "s", "g", "y")

    def __init__(self, nc):
        self.nc = nc
        self.ops = {e: [] for e in self.ENGS}

    def emit(self, eng, fn, wait=(), dma=False, force_wait=()):
        tok = Tok(eng, len(self.ops[eng]), dma)
        if dma:
            tok.marked = True
        waits = []
        for w in wait:
            if w is not None:
                w.marked = True
                waits.append((w, False))
        for w in force_wait:
            if w is not None:
                w.marked = True
                waits.append((w, True))
        self.ops[eng].append((fn, waits, tok, dma))
        return tok

    def make_sems(self):
        self.sems = {}
        for e in self.ENGS:
            if self.ops[e]:
                self.sems[e] = self.nc.semaphore(f"sem_{e}").__enter__()

    def materialize(self, block):
        sems = self.sems
        for e in self.ENGS:
            cnt = 0
            for (fn, waits, tok, dma) in self.ops[e]:
                if tok.marked:
                    cnt += 16 if dma else 1
                    tok.semval = cnt
        engmap = {"t": block.tensor, "v": block.vector, "s": block.scalar,
                  "g": block.gpsimd, "y": block.sync}

        for e in self.ENGS:
            if not self.ops[e]:
                continue
            ops = self.ops[e]

            def make(e=e, ops=ops):
                def body(eng):
                    hiwater = {}
                    pending_throttle = None
                    for (fn, waits, tok, dma) in ops:
                        if dma and pending_throttle is not None:
                            if hiwater.get(e, -1) < pending_throttle:
                                hiwater[e] = pending_throttle
                                eng.wait_ge(sems[e], pending_throttle)
                            pending_throttle = None
                        for (w, forced) in waits:
                            assert w.semval is not None, \
                                f"unmarked token waited on {w.eng}:{w.idx}"
                            if w.eng == e and not w.dma and not forced:
                                continue
                            key = w.eng
                            if hiwater.get(key, -1) >= w.semval:
                                continue
                            hiwater[key] = w.semval
                            eng.wait_ge(sems[w.eng], w.semval)
                        h = fn(eng)
                        if tok.marked:
                            if isinstance(h, tuple):
                                h = h[-1]
                            assert h is not None, f"op on {e} needs handle for inc"
                            h.then_inc(sems[e], 16 if dma else 1)
                            if dma:
                                pending_throttle = tok.semval
                return body
            engmap[e](make())
        return sems


def sbuf(nc, stack, name, shape, dtype):
    return stack.enter_context(nc.sbuf_tensor(name, shape, dtype))


def psum(nc, stack, name, shape, dtype=F32):
    return stack.enter_context(nc.psum_tensor(name, shape, dtype))



def host_constants():
    c = {}
    c["ones_col"] = np.ones((128, 1), np.float32)
    c["ones_row"] = np.ones((1, 128), np.float32)
    c["ones_row_bf"] = np.ones((1, 128), BF)
    c["ones_col_bf"] = np.ones((128, 1), BF)
    c["ones_row_bf64"] = np.ones((1, 64), BF)
    c["ident_f32"] = np.eye(128, dtype=np.float32)
    c["ident_bf"] = np.eye(128, dtype=BF)
    return c


def declare_constants(nc):
    aps = {}
    aps["ones_col"] = nc.dram_tensor("c_ones_col", [128, 1], F32, kind="ExternalInput").ap()
    aps["ones_row"] = nc.dram_tensor("c_ones_row", [1, 128], F32, kind="ExternalInput").ap()
    aps["ones_row_bf"] = nc.dram_tensor("c_ones_row_bf", [1, 128], BF16, kind="ExternalInput").ap()
    aps["ones_col_bf"] = nc.dram_tensor("c_ones_col_bf", [128, 1], BF16, kind="ExternalInput").ap()
    aps["ones_row_bf64"] = nc.dram_tensor("c_ones_row_bf64", [1, 64], BF16, kind="ExternalInput").ap()
    aps["ident_f32"] = nc.dram_tensor("c_ident_f32", [128, 128], F32, kind="ExternalInput").ap()
    aps["ident_bf"] = nc.dram_tensor("c_ident_bf", [128, 128], BF16, kind="ExternalInput").ap()
    return aps


class ConstBufs:

    def __init__(self, E, stack, caps):
        nc = E.nc
        self.onec = sbuf(nc, stack, "cb_onec", [128, 1], F32)
        self.oner = sbuf(nc, stack, "cb_oner", [1, 128], F32)
        self.onerb = sbuf(nc, stack, "cb_onerb", [1, 128], BF16)
        self.onecb = sbuf(nc, stack, "cb_onecb", [128, 1], BF16)
        self.onerb64 = sbuf(nc, stack, "cb_onerb64", [1, 64], BF16)
        self.idf = sbuf(nc, stack, "cb_idf", [128, 128], F32)
        self.idb = sbuf(nc, stack, "cb_idb", [128, 128], BF16)
        toks = []
        for buf, key in [(self.onec, "ones_col"), (self.oner, "ones_row"),
                         (self.onerb, "ones_row_bf"), (self.onecb, "ones_col_bf"),
                         (self.onerb64, "ones_row_bf64"), (self.idf, "ident_f32"),
                         (self.idb, "ident_bf")]:
            toks.append(E.emit("y", (lambda b=buf, k=key: (lambda y: y.dma_start(out=b[:], in_=caps[k])))(), dma=True))
        self.ready = toks[-1]
        for t_ in toks:
            t_.marked = True
        self.all_toks = toks
        nc.const_aps.register(F32, 1.0, self.onec[:, :])



def emit_ln(E, cb, xsrc, xsrc_tok, sq, xn_f32, xn_bf, stat, ba, bb, W,
            extra_war=()):
    toks = {}
    def mk_sq(dch):
        def f(s):
            return s.activation(out=sq[:, dch, :], in_=xsrc[:, dch, :],
                                func=AF.Square)
        return f
    tsq = None
    for dch in range(NDC):
        tsq = E.emit("s", mk_sq(dch),
                     wait=([xsrc_tok] + list(extra_war)) if dch == 0 else ())
    def mk_sum(dch):
        def f(t):
            return t.matmul(ba[0:1, 0:W], cb.onec[:], xsrc[:, dch, :],
                            start=(dch == 0), stop=(dch == NDC - 1))
        return f
    tsum = None
    for dch in range(NDC):
        tsum = E.emit("t", mk_sum(dch),
                      wait=[xsrc_tok, cb.ready] + list(extra_war) if dch == 0 else ())
    def mk_sumsq(dch):
        def f(t):
            return t.matmul(bb[0:1, 0:W], cb.onec[:], sq[:, dch, :],
                            start=(dch == 0), stop=(dch == NDC - 1))
        return f
    tsumsq = None
    for dch in range(NDC):
        tsumsq = E.emit("t", mk_sumsq(dch), wait=[tsq] if dch == 0 else ())
    t1 = E.emit("s", lambda s: s.activation(out=stat[:, 0, :], in_=ba[0:1, 0:W],
                                            func=AF.Copy, scale=1.0 / D),
                wait=[tsum])
    E.emit("s", lambda s: s.activation(out=stat[:, 1, :], in_=bb[0:1, 0:W],
                                       func=AF.Copy, scale=1.0 / D),
           wait=[tsumsq])
    tm2 = E.emit("s", lambda s: s.activation(out=stat[:, 2, :], in_=stat[:, 0, :],
                                             func=AF.Square), force_wait=[t1])
    tvar = E.emit("v", lambda v: v.tensor_tensor(stat[:, 3, :], stat[:, 1, :],
                                                 stat[:, 2, :], op=ALU.subtract),
                  wait=[tm2])
    teps = E.emit("v", lambda v: v.tensor_scalar(stat[:, 3, :], stat[:, 3, :],
                                                 1e-5, None, op0=ALU.add),
                  wait=[tvar], force_wait=[tvar])
    tln = E.emit("s", lambda s: s.activation(out=stat[:, 4, :], in_=stat[:, 3, :],
                                             func=AF.Ln), wait=[teps])
    trstd = E.emit("s", lambda s: s.activation(out=stat[:, 4, :], in_=stat[:, 4, :],
                                               func=AF.Exp, scale=-0.5),
                   force_wait=[tln])
    tbm = E.emit("v", lambda v: v.tensor_tensor(stat[:, 5, :], stat[:, 0, :],
                                                stat[:, 4, :], op=ALU.mult),
                 wait=[trstd])
    tbadd = E.emit("v", lambda v: v.tensor_scalar(stat[:, 5, :], stat[:, 5, :],
                                                  -1.0, None, op0=ALU.mult),
                   force_wait=[tbm])
    E.emit("t", lambda t: t.matmul(ba[:, 0:W], cb.oner[:], stat[:, 4, :],
                                   start=True, stop=True), wait=[trstd, t1, tbadd])
    tbc = E.emit("t", lambda t: t.matmul(bb[:, 0:W], cb.oner[:], stat[:, 5, :],
                                         start=True, stop=True))
    tlast = None
    for dch in range(NDC):
        t_1 = E.emit("v", (lambda dch=dch: lambda v: v.tensor_tensor(
            xn_f32[:, dch, :], xsrc[:, dch, :], ba[:, 0:W], op=ALU.mult))(),
            wait=[tbc, tsumsq] if dch == 0 else (),
            force_wait=[tlast] if tlast is not None else ())
        t_2 = E.emit("v", (lambda dch=dch: lambda v: v.tensor_tensor(
            xn_f32[:, dch, :], xn_f32[:, dch, :], bb[:, 0:W], op=ALU.add))(),
            force_wait=[t_1])
        tlast = E.emit("v", (lambda dch=dch: lambda v: v.tensor_copy(
            xn_bf[:, dch, :], xn_f32[:, dch, :]))(), force_wait=[t_2])
    return tlast



def emit_l1_ffn(E, stack, cb, t_in, t_out, W, tag="ff", nsb_limit=None, dbg=None):
    nc = E.nc
    SB = 512
    NSB = T // SB if nsb_limit is None else nsb_limit
    NF = 16
    NH = SB // 512

    xt = sbuf(nc, stack, tag + "_xt", [128, NDC, SB], F32)
    xnf = sbuf(nc, stack, tag + "_xnf", [128, NDC, 512], F32)
    xn = sbuf(nc, stack, tag + "_xn", [128, NDC, SB], BF16)
    xg = sbuf(nc, stack, tag + "_xg", [128, NDC, SB], BF16)
    hb = sbuf(nc, stack, tag + "_hb", [128, NF, SB], BF16)
    yacc = sbuf(nc, stack, tag + "_yacc", [128, NDC, SB], F32)
    wi = sbuf(nc, stack, tag + "_wi", [128, 2, NDC, 4, 128], BF16)
    ws = sbuf(nc, stack, tag + "_ws", [128, 2, NDC, 4, 128], BF16)
    wo = sbuf(nc, stack, tag + "_wo", [128, NF, D], BF16)
    rt = sbuf(nc, stack, tag + "_rt", [128, 8, 8], F32)
    stat = sbuf(nc, stack, tag + "_stat", [1, 6, 512], F32)
    gtop = sbuf(nc, stack, tag + "_gtop", [128, 8], F32)
    gmask = sbuf(nc, stack, tag + "_gmask", [128, 8], F32)
    gsig = sbuf(nc, stack, tag + "_gsig", [128, 8], F32)
    gtile = sbuf(nc, stack, tag + "_gtile", [128, 8], F32)
    gR = sbuf(nc, stack, tag + "_gR", [1, 8, SB], BF16)
    h1 = sbuf(nc, stack, tag + "_h1", [128, 512], BF16)
    h2 = sbuf(nc, stack, tag + "_h2", [128, 512], BF16)

    bm = [psum(nc, stack, tag + f"_bm{i}", [128, 512]) for i in range(2)]
    bs_ = [psum(nc, stack, tag + f"_bs{i}", [128, 512]) for i in range(2)]
    bo = [psum(nc, stack, tag + f"_bo{i}", [128, 512]) for i in range(2)]
    ba = psum(nc, stack, tag + "_ba", [128, 512])
    bb = psum(nc, stack, tag + "_bb", [128, 512])

    branches = ["sh"] + list(range(8))

    trt = E.emit("y", lambda y: y.dma_start(
        out=rt[:, :, :], in_=W["router"].rearrange("(c p) e -> p c e", p=128)),
        dma=True)

    tok_stored = [None] * NSB
    prev_oacc_last = None
    prev_ms_last = {}
    prev_h_last = None
    prev_branch_done = None
    sc_done = None
    gates_tok = None

    for sb in range(NSB):
        txl = E.emit("y", (lambda sb=sb: lambda y: y.dma_start(
            out=xt[:, :, :],
            in_=t_in[:, sb * SB:(sb + 1) * SB].rearrange(
                "(c p) t -> p c t", p=128)))(), dma=True,
            wait=[tok_stored[sb - 1]] if sb > 0 else ())

        sb_gates_tok = None
        xn_tok = None
        for blk in range(NH):
            off = blk * 512
            xsrc = xt[:, :, off:off + 512]
            xn_view = xn[:, :, off:off + 512]
            war = [w for w in (prev_oacc_last, sc_done) if w is not None]
            xn_tok = emit_ln(E, cb, xsrc, txl, xnf, xnf, xn_view, stat, ba, bb,
                             512, extra_war=war)
            def mk_sc(st4, dch):
                bank = bm[0] if st4 < 2 else bm[1]
                def f(t):
                    return t.matmul(
                        bank[:, (st4 % 2) * 8:(st4 % 2) * 8 + 8],
                        xnf[:, dch, st4 * 128:(st4 + 1) * 128],
                        rt[:, dch, :],
                        start=(dch == 0), stop=(dch == NDC - 1))
                return f
            for st4 in range(4):
                for dch in range(NDC):
                    w = []
                    if st4 == 0 and dch == 0:
                        w = [xn_tok, trt]
                        if prev_h_last is not None:
                            w.append(prev_h_last)
                        if gates_tok is not None:
                            w.append(gates_tok)
                    sc_done = E.emit("t", mk_sc(st4, dch), wait=w)
            prev_tgt = None
            prev_tgr = None
            for st4 in range(4):
                bank = bm[0] if st4 < 2 else bm[1]
                cols = slice((st4 % 2) * 8, (st4 % 2) * 8 + 8)
                tsig = E.emit("s", (lambda bank=bank, cols=cols:
                                    lambda s: s.activation(out=gsig[:],
                                                           in_=bank[:, cols],
                                                           func=AF.Sigmoid))(),
                              wait=[sc_done] + ([prev_tgt] if prev_tgt else []))
                tmax = E.emit("v", (lambda bank=bank, cols=cols:
                                    lambda v: v.max(gtop[:], bank[:, cols]))(),
                              wait=[sc_done] + ([prev_tgr] if prev_tgr is not None else []))
                tmsk = E.emit("v", (lambda bank=bank, cols=cols:
                                    lambda v: v.tensor_scalar(
                                        gmask[:], bank[:, cols], gtop[:, 1:2],
                                        None, op0=ALU.is_ge))(),
                              wait=[sc_done], force_wait=[tmax])
                tgt = E.emit("v", (lambda: lambda v: v.tensor_tensor(
                    gtile[:], gmask[:], gsig[:], op=ALU.mult))(),
                    wait=[tsig], force_wait=[tmsk])
                prev_tgt = tgt
                def mk_gtr(e):
                    rb = ba if e < 4 else bb
                    def f(t):
                        return t.matmul(rb[0:1, (e % 4) * 128:(e % 4) * 128 + 128],
                                        gtile[:, e:e + 1], cb.idf[:],
                                        start=True, stop=True)
                    return f
                tgr = None
                for e in range(8):
                    w = [tgt, cb.ready]
                    if e == 0 and gates_tok is not None:
                        w.append(gates_tok)
                    tgr = E.emit("t", mk_gtr(e), wait=w)
                prev_tgr = tgr
                def mk_gcp(st4=st4, off=off):
                    def f(v):
                        h = None
                        for e in range(8):
                            rb = ba if e < 4 else bb
                            h = v.tensor_copy(
                                gR[0:1, e, off + st4 * 128:off + st4 * 128 + 128],
                                rb[0:1, (e % 4) * 128:(e % 4) * 128 + 128])
                        return h
                    return f
                gates_tok = E.emit("v", mk_gcp(), wait=[tgr])
        sb_gates_tok = gates_tok
        tyz = E.emit("v", lambda v: v.memset(yacc[:, :, :], 0.0),
                     wait=[tok_stored[sb - 1]] if sb > 0 else ())
        oacc_last = None
        for br in branches:
            if br != "sh":
                def mk_gb(half, br=br):
                    def f(t):
                        return t.matmul(
                            bo[half][:, :], cb.onerb[:],
                            gR[0:1, br, half * 512:(half + 1) * 512],
                            start=True, stop=True)
                    return f
                w0 = [sb_gates_tok]
                if oacc_last is not None:
                    w0.append(oacc_last)
                elif prev_oacc_last is not None:
                    w0.append(prev_oacc_last)
                tgb = None
                for half in range(NH):
                    tgb = E.emit("t", mk_gb(half), wait=w0 if half == 0 else ())
                txg = None
                for dch in range(NDC):
                    def mk_xg(dch=dch):
                        def f(v):
                            h = None
                            for half in range(NH):
                                h = v.tensor_tensor(
                                    xg[:, dch, half * 512:(half + 1) * 512],
                                    xn[:, dch, half * 512:(half + 1) * 512],
                                    bo[half][:, :], op=ALU.mult)
                            return h
                        return f
                    txg = E.emit("v", mk_xg(),
                                 wait=[tgb, xn_tok] if dch == 0 else ())
                src = xg
            else:
                txg = xn_tok
                src = xn
            h_toks = []
            twl = twol = None
            for f_ in range(NF):
                if br == "sh":
                    win_src, wsw_src, wo_src = (W["wsh_in_T"], W["wsh_sw_T"],
                                                W["wsh_out_T"])
                else:
                    win_src, wsw_src, wo_src = (W["w_in_e"][br], W["w_sw_e"][br],
                                                W["w_out_e"][br])
                if f_ % 4 == 0:
                    fg = f_ // 4
                    gp = fg % 2
                    wld_wait = []
                    if gp in prev_ms_last:
                        wld_wait.append(prev_ms_last[gp])
                    def mk_wld(buf, srcw, f_=f_, gp=gp):
                        def f(g):
                            return g.dma_start(
                                out=buf[:, gp, :, :, :],
                                in_=srcw[:, f_ * 128:(f_ + 4) * 128].rearrange(
                                    "(c p) f -> p c f", p=128))
                        return f
                    E.emit("g", mk_wld(wi, win_src), dma=True, wait=wld_wait)
                    twl = E.emit("g", mk_wld(ws, wsw_src), dma=True)
                    twol = E.emit("g", (lambda f_=f_, wo_src=wo_src: lambda g:
                                        g.dma_start(
                                            out=wo[:, f_:f_ + 4, :],
                                            in_=wo_src[f_ * 128:(f_ + 4) * 128, :].rearrange(
                                                "(g p) d -> p g d", p=128)))(),
                                  dma=True,
                                  wait=[prev_branch_done]
                                  if (f_ == 0 and prev_branch_done is not None) else ())
                gp = (f_ // 4) % 2
                gidx = f_ % 4
                def mk_ms(bank, wsrc, half, gp=gp, gidx=gidx, src=src):
                    def f(t):
                        h = None
                        for dch in range(NDC):
                            h = t.matmul(bank[:, :], wsrc[:, gp, dch, gidx, :],
                                         src[:, dch, half * 512:(half + 1) * 512],
                                         start=(dch == 0), stop=(dch == NDC - 1))
                        return h
                    return f
                msw = [twl, txg]
                if f_ == 0:
                    if prev_h_last is not None:
                        msw.append(prev_h_last)
                    msw.append(sb_gates_tok)
                if f_ >= 1:
                    msw.append(h_toks[f_ - 1])
                tms = None
                for half in range(NH):
                    E.emit("t", mk_ms(bm[half], wi, half),
                           wait=msw if half == 0 else ())
                for half in range(NH):
                    tms = E.emit("t", mk_ms(bs_[half], ws, half))
                if f_ % 4 == 3:
                    prev_ms_last[(f_ // 4) % 2] = tms
                hsil = [h1, h2]
                def mk_sl(half):
                    def f(s):
                        return s.activation(out=hsil[half][:], in_=bm[half][:, :],
                                            func=AF.Silu)
                    return f
                tsl = None
                for half in range(NH):
                    tsl = E.emit("s", mk_sl(half),
                                 wait=([tms] + ([h_toks[f_ - 1]] if f_ >= 1 else
                                               ([prev_h_last] if prev_h_last is not None else [])))
                                 if half == 0 else ())
                def mk_h(f_=f_):
                    def f(v):
                        h = None
                        for half in range(NH):
                            h = v.tensor_tensor(
                                hb[:, f_, half * 512:(half + 1) * 512],
                                hsil[half][:], bs_[half][:, :], op=ALU.mult)
                        return h
                    return f
                th = E.emit("v", mk_h(), wait=[tsl])
                h_toks.append(th)
            prev_h_last = h_toks[-1]
            for dch in range(NDC):
                def mk_out(half, dch=dch):
                    def f(t):
                        h = None
                        for f_ in range(NF):
                            h = t.matmul(bo[half][:, :],
                                         wo[:, f_, dch * 128:(dch + 1) * 128],
                                         hb[:, f_, half * 512:(half + 1) * 512],
                                         start=(f_ == 0), stop=(f_ == NF - 1))
                        return h
                    return f
                w0 = [h_toks[-1], twol]
                if oacc_last is not None:
                    w0.append(oacc_last)
                else:
                    w0.append(txg if br != "sh" else sb_gates_tok)
                    if prev_oacc_last is not None:
                        w0.append(prev_oacc_last)
                tob = None
                for half in range(NH):
                    tob = E.emit("t", mk_out(half, dch),
                                 wait=w0 if half == 0 else ())
                def mk_oa(dch=dch):
                    def f(v):
                        h = None
                        for half in range(NH):
                            h = v.tensor_tensor(
                                yacc[:, dch, half * 512:(half + 1) * 512],
                                yacc[:, dch, half * 512:(half + 1) * 512],
                                bo[half][:, :], op=ALU.add)
                        return h
                    return f
                oacc_last = E.emit("v", mk_oa(), wait=[tob, tyz])
            prev_branch_done = oacc_last
        prev_oacc_last = oacc_last
        if dbg is not None and sb == 0:
            E.emit("y", lambda y: y.dma_start(out=dbg["xn"], in_=xn[:, :, :]),
                   dma=True, wait=[xn_tok])
            E.emit("y", lambda y: y.dma_start(out=dbg["gR"], in_=gR[:, :, :]),
                   dma=True, wait=[sb_gates_tok])
            E.emit("y", lambda y: y.dma_start(out=dbg["hb"], in_=hb[:, :, :]),
                   dma=True, wait=[prev_h_last])
            E.emit("y", lambda y: y.dma_start(out=dbg["yacc"], in_=yacc[:, :, :]),
                   dma=True, wait=[oacc_last])
        dbg_tok = E.ops["y"][-1][2] if (dbg is not None and sb == 0) else None
        tres = None
        for dch in range(NDC):
            tres = E.emit("v", (lambda dch=dch: lambda v: v.tensor_tensor(
                yacc[:, dch, :], yacc[:, dch, :], xt[:, dch, :], op=ALU.add))(),
                wait=([oacc_last] + ([dbg_tok] if dbg_tok is not None else []))
                if dch == 0 else ())
        tst = E.emit("y", (lambda sb=sb: lambda y: y.dma_start(
            out=t_out[:, sb * SB:(sb + 1) * SB].rearrange(
                "(c p) t -> p c t", p=128),
            in_=yacc[:, :, :]))(), dma=True, wait=[tres])
        tok_stored[sb] = tst

    return tok_stored[-1]



def emit_l1_attn_b(E, stack, cb, t_in, t_out, W, xn_dram, tag="ab"):
    nc = E.nc
    H, DH = 16, 64
    TB1 = 256
    NTB1 = T // TB1
    TB = 512
    NTB = T // TB

    xt = sbuf(nc, stack, tag + "_xt", [128, NDC, TB1], F32)
    xnf = sbuf(nc, stack, tag + "_xnf", [128, NDC, TB1], F32)
    xnb = sbuf(nc, stack, tag + "_xnb", [128, NDC, TB1], BF16)
    stat = sbuf(nc, stack, tag + "_stat", [1, 6, TB1], F32)
    gam = sbuf(nc, stack, tag + "_gam", [128, NDC], F32)
    bet = sbuf(nc, stack, tag + "_bet", [128, NDC], F32)
    kfm = sbuf(nc, stack, tag + "_kfm", [128, 8, T], BF16)
    vtok = sbuf(nc, stack, tag + "_vtok", [128, 32, 1024], BF16)
    ws1 = sbuf(nc, stack, tag + "_ws1", [128, 2, NDC, 128], BF16)
    ws2 = sbuf(nc, stack, tag + "_ws2", [128, 2, NDC, 128], BF16)
    wmv = sbuf(nc, stack, tag + "_wmv", [128, 2, TB], BF16)
    qb = sbuf(nc, stack, tag + "_qb", [128, 8, TB], BF16)
    pt = sbuf(nc, stack, tag + "_pt", [128, 2, TB], BF16)
    mq = sbuf(nc, stack, tag + "_mq", [128, 4, TB], BF16)
    mk = sbuf(nc, stack, tag + "_mk", [128, 4, TB], BF16)
    mxp = sbuf(nc, stack, tag + "_mxp", [128, 8], F32)
    mxc = sbuf(nc, stack, tag + "_mxc", [128, 1], F32)
    nmr = sbuf(nc, stack, tag + "_nmr", [1, TB], BF16)
    srec = sbuf(nc, stack, tag + "_srec", [1, TB], BF16)
    srl = sbuf(nc, stack, tag + "_srl", [1, TB], F32)
    rsb = sbuf(nc, stack, tag + "_rsb", [128, TB], BF16)
    ctx = sbuf(nc, stack, tag + "_ctx", [128, NDC, TB], BF16)
    yac = sbuf(nc, stack, tag + "_yac", [128, NDC, TB1], F32)

    b0 = psum(nc, stack, tag + "_b0", [128, TB])
    b1 = psum(nc, stack, tag + "_b1", [128, TB])
    b2 = psum(nc, stack, tag + "_b2", [128, TB])
    b3 = psum(nc, stack, tag + "_b3", [128, TB])
    ba = psum(nc, stack, tag + "_ba", [128, TB])
    bb = psum(nc, stack, tag + "_bb", [128, TB])

    tg = E.emit("y", lambda y: y.dma_start(
        out=gam[:, :], in_=W["ln_gamma"].rearrange("(c p) o -> p (c o)", p=128),
        allow_slow_non_contiguous=True), dma=True)
    tb = E.emit("y", lambda y: y.dma_start(
        out=bet[:, :], in_=W["ln_beta"].rearrange("(c p) o -> p (c o)", p=128),
        allow_slow_non_contiguous=True), dma=True)
    tmq = E.emit("y", lambda y: y.dma_start(
        out=mq[:, :, :], in_=W["masks_q"].rearrange("m p t -> p m t")), dma=True)
    tmk = E.emit("y", lambda y: y.dma_start(
        out=mk[:, :, :], in_=W["masks_k"].rearrange("m p t -> p m t")), dma=True)

    def wslice_load(buf, p, wsrc, mt, waits):
        return E.emit("g", (lambda buf=buf, p=p, wsrc=wsrc, mt=mt: lambda g:
                            g.dma_start(out=buf[:, p, :, :],
                                        in_=wsrc[:, mt * 128:(mt + 1) * 128].rearrange(
                                            "(c p) e -> p c e", p=128)))(),
                      dma=True, wait=waits)

    prev_store = None
    prev_kv = None
    kq_last_use = {}
    wv_last_use = {}
    for tb_ in range(NTB1):
        txl = E.emit("y", (lambda tb_=tb_: lambda y: y.dma_start(
            out=xt[:, :, :],
            in_=t_in[:, tb_ * TB1:(tb_ + 1) * TB1].rearrange("(c p) t -> p c t", p=128)))(),
            dma=True, wait=[prev_store] if prev_store is not None else ())
        war = [prev_kv] if prev_kv is not None else []
        xn_tok = emit_ln(E, cb, xt[:, :, :], txl, xnf, xnf, xnb, stat, ba, bb,
                         TB1, extra_war=war + [tg, tb])
        aff = []
        for dch in range(NDC):
            taf = E.emit("v", (lambda dch=dch: lambda v: v.tensor_scalar(
                xnb[:, dch, :], xnf[:, dch, :], gam[:, dch:dch + 1],
                bet[:, dch:dch + 1], op0=ALU.mult, op1=ALU.add))(),
                force_wait=[xn_tok] if dch == 0 else ())
            aff.append(taf)
        taff = E.emit("v", lambda v: v.tensor_copy(xnb[:, 0, 0:1], xnb[:, 0, 0:1]),
                      force_wait=aff)
        txs = E.emit("y", (lambda tb_=tb_: lambda y: y.dma_start(
            out=xn_dram[:, tb_ * TB1:(tb_ + 1) * TB1].rearrange("(c p) t -> p c t", p=128),
            in_=xnb[:, :, :]))(), dma=True, wait=[taff])
        kv_ev = None
        for mt in range(8):
            p = mt % 2
            twl = wslice_load(ws1, p, W["wkT"], mt,
                              [kq_last_use.get(p)] if kq_last_use.get(p) else [])
            def mk_k(mt=mt, p=p):
                def f(t):
                    h = None
                    for dch in range(NDC):
                        h = t.matmul(b0[:, 0:TB1], ws1[:, p, dch, :],
                                     xnb[:, dch, :], start=(dch == 0),
                                     stop=(dch == NDC - 1))
                    return h
                return f
            w = [taff, twl]
            if prev_kv is not None and mt == 0:
                w.append(prev_kv)
            if mt > 0:
                w.append(kv_ev)
            tkm = E.emit("t", mk_k(), wait=w)
            kq_last_use[p] = tkm
            kv_ev = E.emit("v", (lambda mt=mt, tb_=tb_: lambda v: v.tensor_copy(
                kfm[:, mt, tb_ * TB1:(tb_ + 1) * TB1], b0[:, 0:TB1]))(), wait=[tkm])
        vbanks = [b1, b2]
        prev_vev = kv_ev
        for half in range(2):
            tvms = [None, None]
            for dch in range(NDC):
                p = dch % 2
                twv = E.emit("g", (lambda p=p, dch=dch, half=half: lambda g:
                                   g.dma_start(out=wmv[:, p, :],
                                               in_=W["wvT"][dch * 128:(dch + 1) * 128,
                                                            half * 512:(half + 1) * 512]))(),
                             dma=True,
                             wait=[wv_last_use.get(p)] if wv_last_use.get(p) else [])
                for st2 in range(2):
                    w = [taff, twv]
                    if dch == 0 and st2 == 0:
                        w.append(prev_vev)
                    tvm = E.emit("t", (lambda st2=st2, dch=dch, p=p, bank=vbanks[st2]:
                                       lambda t: t.matmul(
                        bank[:, :], xnb[:, dch, st2 * 128:(st2 + 1) * 128],
                        wmv[:, p, :], start=(dch == 0), stop=(dch == NDC - 1)))(),
                        wait=w)
                    tvms[st2] = tvm
                wv_last_use[p] = tvms[1]
            for st2 in range(2):
                prev_vev = E.emit("v", (lambda st2=st2, half=half, tb_=tb_,
                                        bank=vbanks[st2]: lambda v: v.tensor_copy(
                    vtok[:, tb_ * 2 + st2, half * 512:(half + 1) * 512],
                    bank[:, :]))(), wait=[tvms[st2]])
        prev_kv = prev_vev
        prev_store = txs
    kv_done = prev_kv

    prev_yst = None
    for qb_ in range(NTB):
        q_toks = []
        txq = None
        for hh in range(2):
            txq = E.emit("y", (lambda qb_=qb_, hh=hh: lambda y: y.dma_start(
                out=xnb[:, :, :],
                in_=xn_dram[:, qb_ * TB + hh * TB1:qb_ * TB + (hh + 1) * TB1]
                .rearrange("(c p) t -> p c t", p=128)))(),
                dma=True,
                wait=([prev_yst] if prev_yst is not None else [kv_done])
                if hh == 0 else [q_toks[-1]])
            for mt in range(8):
                p = mt % 2
                twl = wslice_load(ws1, p, W["wqT"], mt,
                                  [kq_last_use.get(p)] if kq_last_use.get(p) else [])
                def mk_q(mt=mt, p=p):
                    def f(t):
                        h = None
                        for dch in range(NDC):
                            h = t.matmul(b0[:, 0:TB1], ws1[:, p, dch, :],
                                         xnb[:, dch, :], start=(dch == 0),
                                         stop=(dch == NDC - 1))
                        return h
                    return f
                w = [txq, twl]
                if q_toks:
                    w.append(q_toks[-1])
                elif prev_yst is not None:
                    w.append(prev_yst)
                else:
                    w.append(kv_done)
                tqm = E.emit("t", mk_q(), wait=w)
                kq_last_use[p] = tqm
                q_toks.append(E.emit("s", (lambda mt=mt, hh=hh: lambda s: s.activation(
                    out=qb[:, mt, hh * TB1:(hh + 1) * TB1], in_=b0[:, 0:TB1],
                    func=AF.Copy, scale=0.125))(), wait=[tqm]))
        tq_done = q_toks[-1]
        nkt = (qb_ + 1) * 4
        nkb = qb_ + 1
        prev_ctx_ev = None
        prev_tcp = None
        for h_ in range(H):
            mt, lo = h_ // 2, (h_ % 2) * 64
            hsl = slice(lo, lo + 64)
            sc_toks = {}
            for st4 in range(4):
                q0 = qb_ * TB + st4 * 128
                for kb in range(nkb):
                    if kb * 512 > q0 + 127:
                        continue
                    bank = b0 if kb % 2 == 0 else b1
                    def mk_s(st4=st4, kb=kb, bank=bank, mt=mt, hsl=hsl):
                        def f(t):
                            return t.matmul(
                                bank[:, :],
                                qb[:, mt, st4 * 128:(st4 + 1) * 128][hsl, :],
                                kfm[:, mt, kb * 512:(kb + 1) * 512][hsl, :],
                                start=True, stop=True)
                        return f
                    w = [tq_done]
                    prev_use = sc_toks.get((st4, kb - 2))
                    if prev_use is not None:
                        w.append(prev_use[1])
                    if prev_ctx_ev is not None:
                        w.append(prev_ctx_ev)
                    tsm = E.emit("t", mk_s(), wait=w)
                    moff = (q0 - kb * 512) // 128
                    if 0 <= moff < 4:
                        tsm = E.emit("v", (lambda bank=bank, moff=moff:
                                           lambda v: v.tensor_tensor(
                                               bank[:, :], bank[:, :],
                                               mq[:, moff, :], op=ALU.add))(),
                                     wait=[tsm, tmq])
                    tmx = E.emit("v", (lambda bank=bank, kb=kb: lambda v:
                                       v.reduce_max(mxp[:, kb:kb + 1], bank[:, :],
                                                    axis=AX.X))(), wait=[tsm])
                    sc_toks[(st4, kb)] = (tsm, tmx)
                nkb_st = q0 // 512 + 1
                tcm = E.emit("v", (lambda nkb_st=nkb_st: lambda v: v.reduce_max(
                    mxc[:, :], mxp[:, 0:nkb_st], axis=AX.X))(),
                    wait=[sc_toks[(st4, q0 // 512)][1]])
                tneg = E.emit("v", lambda v: v.tensor_scalar(
                    mxc[:, :], mxc[:, :], -1.0, None, op0=ALU.mult),
                    force_wait=[tcm])
                trow = E.emit("t", (lambda st4=st4: lambda t: t.matmul(
                    bb[0:1, st4 * 128:(st4 + 1) * 128], mxc[:, :], cb.idf[:],
                    start=True, stop=True))(),
                    wait=[tneg, cb.ready]
                    + ([prev_ctx_ev] if prev_ctx_ev is not None else []))
                tnm = E.emit("v", (lambda st4=st4: lambda v: v.tensor_copy(
                    nmr[:, st4 * 128:(st4 + 1) * 128],
                    bb[0:1, st4 * 128:(st4 + 1) * 128]))(), wait=[trow])
                sc_toks[("nm", st4)] = tnm
            tnm_all = sc_toks[("nm", 3)]
            tsum_prev = None
            ctx_mm_last = None
            pt_free = [None, None]
            for kt in range(nkt):
                bank = b0 if kt % 2 == 0 else b1
                def mk_st(kt=kt, bank=bank, mt=mt, hsl=hsl):
                    def f(t):
                        return t.matmul(
                            bank[:, :],
                            kfm[:, mt, kt * 128:(kt + 1) * 128][hsl, :],
                            qb[:, mt, :][hsl, :], start=True, stop=False)
                    return f
                tst = E.emit("t", mk_st(), wait=[tnm_all])
                tst = E.emit("t", (lambda bank=bank: lambda t: t.matmul(
                    bank[:, :], cb.onerb[:], nmr[:, :], start=False, stop=True))())
                moff = kt - qb_ * 4
                if 0 <= moff < 4:
                    tst = E.emit("v", (lambda bank=bank, moff=moff:
                                       lambda v: v.tensor_tensor(
                                           bank[:, :], bank[:, :],
                                           mk[:, moff, :], op=ALU.add))(),
                                 wait=[tst, tmk])
                wexp = [tst]
                if pt_free[kt % 2] is not None:
                    wexp.append(pt_free[kt % 2])
                texp = E.emit("s", (lambda kt=kt, bank=bank: lambda s: s.activation(
                    out=pt[:, kt % 2, :], in_=bank[:, :], func=AF.Exp))(),
                    wait=wexp)
                tsum_prev = E.emit("t", (lambda kt=kt, nkt=nkt: lambda t: t.matmul(
                    b3[0:1, :], cb.onecb[:], pt[:, kt % 2, :],
                    start=(kt == 0), stop=(kt == nkt - 1)))(), wait=[texp])
                ctx_mm_last = E.emit("t", (lambda kt=kt, nkt=nkt, h_=h_:
                                           lambda t: t.matmul(
                    b2[0:64, :], vtok[:, kt, h_ * 64:(h_ + 1) * 64],
                    pt[:, kt % 2, :], start=(kt == 0), stop=(kt == nkt - 1)))())
                pt_free[kt % 2] = ctx_mm_last
            trc1 = E.emit("s", lambda s: s.activation(out=srl[:, :],
                                                      in_=b3[0:1, :], func=AF.Ln),
                          wait=[tsum_prev])
            trc2 = E.emit("s", lambda s: s.activation(out=srec[:, :],
                                                      in_=srl[:, :], func=AF.Exp,
                                                      scale=-1.0), force_wait=[trc1])
            tbc = E.emit("t", lambda t: t.matmul(ba[0:64, :], cb.onerb64[:],
                                                 srec[:, :], start=True, stop=True),
                         wait=[trc2] + ([prev_tcp] if prev_tcp is not None else []))
            tcp = E.emit("v", lambda v: v.tensor_copy(rsb[0:64, :], ba[0:64, :]),
                         wait=[tbc])
            prev_tcp = tcp
            cmt, clo = h_ // 2, (h_ % 2) * 64
            prev_ctx_ev = E.emit("v", (lambda cmt=cmt, clo=clo: lambda v:
                                       v.tensor_tensor(
                ctx[clo:clo + 64, cmt, :],
                b2[0:64, :], rsb[0:64, :], op=ALU.mult))(),
                wait=[ctx_mm_last, tcp])
        ty_last = None
        wo_last = {}
        for hh in range(2):
            txr = E.emit("y", (lambda qb_=qb_, hh=hh: lambda y: y.dma_start(
                out=xt[:, :, :],
                in_=t_in[:, qb_ * TB + hh * TB1:qb_ * TB + (hh + 1) * TB1]
                .rearrange("(c p) t -> p c t", p=128)))(),
                dma=True,
                wait=[prev_yst] if (prev_yst is not None and hh == 0) else
                ([ty_last] if hh == 1 else ()))
            for mt in range(8):
                p = mt % 2
                twl = wslice_load(ws2, p, W["woT"], mt,
                                  [wo_last.get(p)] if wo_last.get(p) else [])
                def mk_o(mt=mt, p=p, hh=hh):
                    def f(t):
                        h = None
                        for dch in range(NDC):
                            h = t.matmul(b1[:, 0:TB1], ws2[:, p, dch, :],
                                         ctx[:, dch, hh * TB1:(hh + 1) * TB1],
                                         start=(dch == 0), stop=(dch == NDC - 1))
                        return h
                    return f
                w = [prev_ctx_ev, twl]
                if ty_last is not None:
                    w.append(ty_last)
                tom = E.emit("t", mk_o(), wait=w)
                wo_last[p] = tom
                ty_last = E.emit("v", (lambda mt=mt: lambda v: v.tensor_tensor(
                    yac[:, mt, :], b1[:, 0:TB1], xt[:, mt, :], op=ALU.add))(),
                    wait=[tom, txr])
            prev_yst = E.emit("y", (lambda qb_=qb_, hh=hh: lambda y: y.dma_start(
                out=t_out[:, qb_ * TB + hh * TB1:qb_ * TB + (hh + 1) * TB1]
                .rearrange("(c p) t -> p c t", p=128),
                in_=yac[:, :, :]))(), dma=True, wait=[ty_last])
    return prev_yst



def emit_l0_ffn(E, stack, cb, t_in, t_out, W, tag="tf"):
    nc = E.nc
    TB = 512
    NTB = T // TB
    NM = 32

    xt = sbuf(nc, stack, tag + "_xt", [128, NDC, TB], F32)
    xnf = sbuf(nc, stack, tag + "_xnf", [128, NDC, TB], F32)
    xnb = sbuf(nc, stack, tag + "_xnb", [128, NDC, TB], BF16)
    stat = sbuf(nc, stack, tag + "_stat", [1, 6, TB], F32)
    ctrl = sbuf(nc, stack, tag + "_ctrl", [128, NDC, 64], F32)
    tier = sbuf(nc, stack, tag + "_tier", [1, TB], F32)
    onr64 = sbuf(nc, stack, tag + "_onr64", [1, 64], F32)
    mxg = sbuf(nc, stack, tag + "_mxg", [64, 32], F32)
    perm = sbuf(nc, stack, tag + "_perm", [64, TB], BF16)
    esel = sbuf(nc, stack, tag + "_esel", [64, NM, 128], BF16)
    biasc = sbuf(nc, stack, tag + "_biasc", [128, NM], F32)
    f1s = sbuf(nc, stack, tag + "_f1s", [128, 2, NDC, 128], BF16)
    f2s = sbuf(nc, stack, tag + "_f2s", [128, 2, 4, D], BF16)
    inn = sbuf(nc, stack, tag + "_inn", [128, TB], BF16)
    mih = sbuf(nc, stack, tag + "_mih", [128, NM, TB], BF16)
    yac = sbuf(nc, stack, tag + "_yac", [128, NDC, TB], F32)

    ba = psum(nc, stack, tag + "_ba", [128, TB])
    bb = psum(nc, stack, tag + "_bb", [128, TB])
    bl = psum(nc, stack, tag + "_bl", [128, TB])
    bi = [psum(nc, stack, tag + f"_bi{i}", [128, TB]) for i in range(2)]
    bp = psum(nc, stack, tag + "_bp", [128, TB])
    bo = [psum(nc, stack, tag + f"_bo{i}", [128, TB]) for i in range(2)]

    tld = []
    tld.append(E.emit("y", lambda y: y.dma_start(
        out=ctrl[:, :, :], in_=W["ctrl"].rearrange("(c p) e -> p c e", p=128)), dma=True))
    tld.append(E.emit("y", lambda y: y.dma_start(out=tier[:, :], in_=W["tie_row"]), dma=True))
    tld.append(E.emit("y", lambda y: y.dma_start(out=onr64[:, :], in_=W["ones_row64"]), dma=True))
    tld.append(E.emit("y", lambda y: y.dma_start(
        out=esel[:, :, :], in_=W["esel"]), dma=True))
    tld.append(E.emit("y", lambda y: y.dma_start(
        out=biasc[:, :], in_=W["bias_col"].rearrange("m p -> p m")
        if False else W["bias_col"]), dma=True))
    tcst = tld[-1]

    prev_store = None
    prev_inner_last = None
    prev_perm_use = None
    prev_out_last = None
    f1_last_use = {}
    f2_last_use = {}
    for tb_ in range(NTB):
        txl = E.emit("y", (lambda tb_=tb_: lambda y: y.dma_start(
            out=xt[:, :, :],
            in_=t_in[:, tb_ * TB:(tb_ + 1) * TB].rearrange("(c p) t -> p c t", p=128)))(),
            dma=True, wait=[prev_store] if prev_store is not None else ())
        war = [w for w in (prev_inner_last,) if w is not None]
        xn_tok = emit_ln(E, cb, xt[:, :, :], txl, xnf, xnf, xnb, stat, ba, bb,
                         TB, extra_war=war)
        tlg = None
        for dch in range(NDC):
            tlg = E.emit("t", (lambda dch=dch: lambda t: t.matmul(
                bl[0:64, :], ctrl[:, dch, :], xnf[:, dch, :],
                start=(dch == 0), stop=False))(),
                wait=([xn_tok, tcst] + ([prev_perm_use] if prev_perm_use is not None else []))
                if dch == 0 else ())
        tlg = E.emit("t", lambda t: t.matmul(bl[0:64, :], onr64[:, :], tier[:, :],
                                             start=False, stop=True))
        tmx = E.emit("v", lambda v: v.tensor_reduce(
            mxg[:, :], bl[0:64, :].rearrange("p (g u) -> p g u", u=16),
            op=ALU.max, axis=AX.X), wait=[tlg])
        mxap = mxg[:, :]
        mxb = AP(mxap.tensor, 0, [[mxap.ap[0][0], 64], [1, 32], [0, 16]])
        tpm = E.emit("v", (lambda mxb=mxb: lambda v: v.tensor_tensor(
            perm[:, :], bl[0:64, :].rearrange("p (g u) -> p g u", u=16), mxb,
            op=ALU.is_equal))(), force_wait=[tmx])
        inner_toks = []
        for m in range(NM):
            p = m % 2
            twf = E.emit("g", (lambda m=m, p=p: lambda g: g.dma_start(
                out=f1s[:, p, :, :],
                in_=W["f1"][:, m * 128:(m + 1) * 128].rearrange(
                    "(c p) e -> p c e", p=128)))(), dma=True,
                wait=[f1_last_use[p]] if p in f1_last_use else [])
            def mk_in(m=m, p=p):
                def f(t):
                    h = None
                    for dch in range(NDC):
                        h = t.matmul(bi[p][:, :], f1s[:, p, dch, :],
                                     xnb[:, dch, :], start=(dch == 0),
                                     stop=(dch == NDC - 1))
                    return h
                return f
            w = [twf, xn_tok]
            if m >= 2:
                w.append(inner_toks[m - 2])
            elif prev_inner_last is not None:
                w.append(prev_inner_last)
            tin = E.emit("t", mk_in(), wait=w)
            f1_last_use[p] = tin
            trl = E.emit("s", (lambda m=m, p=p: lambda s: s.activation(
                out=inn[:, :], in_=bi[p][:, :], func=AF.Relu,
                bias=biasc[:, m:m + 1]))(),
                wait=[tin] + ([inner_toks[m - 1]] if m >= 1 else
                              ([prev_inner_last] if prev_inner_last is not None else [])))
            tpb = E.emit("t", (lambda m=m: lambda t: t.matmul(
                bp[:, :], esel[:, m, :], perm[:, :], start=True, stop=True))(),
                wait=[tpm] + ([inner_toks[m - 1]] if m >= 1 else
                              ([prev_inner_last] if prev_inner_last is not None else [])))
            tmi = E.emit("v", (lambda m=m: lambda v: v.tensor_tensor(
                mih[:, m, :], inn[:, :], bp[:, :], op=ALU.mult))(),
                wait=[trl, tpb], force_wait=[trl])
            inner_toks.append(tmi)
        prev_inner_last = inner_toks[-1]
        out_toks = []
        for dt in range(NDC):
            tob = None
            for m in range(NM):
                p = (m // 4) % 2
                if m % 4 == 0:
                    wld = [f2_last_use[p]] if p in f2_last_use else []
                    tw2 = E.emit("g", (lambda m=m, p=p, dt=dt: lambda g: g.dma_start(
                        out=f2s[:, p, :, :],
                        in_=W["f2"][m * 128:(m + 4) * 128, :].rearrange(
                            "(g p) d -> p g d", p=128)))(), dma=True, wait=wld)
                w = [inner_toks[-1], tw2]
                if m == 0:
                    if out_toks:
                        w.append(out_toks[-1])
                    elif prev_out_last is not None:
                        w.append(prev_out_last)
                tob = E.emit("t", (lambda m=m, p=p, dt=dt: lambda t: t.matmul(
                    bo[dt % 2][:, :], f2s[:, p, m % 4, dt * 128:(dt + 1) * 128],
                    mih[:, m, :], start=(m == 0), stop=(m == NM - 1)))(), wait=w)
                f2_last_use[p] = tob
            tev = E.emit("v", (lambda dt=dt: lambda v: v.tensor_tensor(
                yac[:, dt, :], bo[dt % 2][:, :], xt[:, dt, :], op=ALU.add))(),
                wait=[tob])
            out_toks.append(tev)
        prev_out_last = out_toks[-1]
        prev_perm_use = inner_toks[-1]
        tst = E.emit("y", (lambda tb_=tb_: lambda y: y.dma_start(
            out=t_out[:, tb_ * TB:(tb_ + 1) * TB].rearrange("(c p) t -> p c t", p=128),
            in_=yac[:, :, :]))(), dma=True, wait=[out_toks[-1]])
        prev_store = tst
    return prev_store


def prep_l0_ffn_weights(inputs):
    w = {}
    w["ctrl"] = np.ascontiguousarray(inputs["L0_ffn_controller"].reshape(1024, 64))
    w["f1"] = np.ascontiguousarray(inputs["L0_ffn_f1"].reshape(1024, 4096))
    w["f2"] = np.ascontiguousarray(inputs["L0_ffn_f2"].reshape(4096, 1024))
    w["bias_col"] = np.ascontiguousarray(inputs["L0_ffn_bias"].reshape(32, 128).T)
    tie = np.linspace(0.0, 1e-6, 16, dtype=np.float32)
    w["tie_row"] = np.tile(tie, 32)[None, :].astype(np.float32)
    w["ones_row64"] = np.ones((1, 64), np.float32)
    es = np.zeros((64, 32, 128), BF)
    for m in range(32):
        for pp in range(128):
            es[2 * m + pp // 64, m, pp] = 1.0
    w["esel"] = es
    return w


def build_l0_ffn_program():
    from contextlib import ExitStack
    nc = bass.Bass("TRN2", target_bir_lowering=False)
    t_in = nc.dram_tensor("x_fm", [D, T], F32, kind="ExternalInput").ap()
    t_out = nc.dram_tensor("y_fm", [D, T], F32, kind="ExternalOutput").ap()
    W = {}
    W["ctrl"] = nc.dram_tensor("ctrl", [D, 64], F32, kind="ExternalInput").ap()
    W["f1"] = nc.dram_tensor("f1", [D, 4096], F32, kind="ExternalInput").ap()
    W["f2"] = nc.dram_tensor("f2", [4096, D], F32, kind="ExternalInput").ap()
    W["bias_col"] = nc.dram_tensor("bias_col", [128, 32], F32, kind="ExternalInput").ap()
    W["tie_row"] = nc.dram_tensor("tie_row", [1, 512], F32, kind="ExternalInput").ap()
    W["ones_row64"] = nc.dram_tensor("ones_row64", [1, 64], F32, kind="ExternalInput").ap()
    W["esel"] = nc.dram_tensor("esel", [64, 32, 128], BF16, kind="ExternalInput").ap()
    caps = declare_constants(nc)
    stack = ExitStack()
    with stack:
        E = Emitter(nc)
        cb = ConstBufs(E, stack, caps)
        tlast = emit_l0_ffn(E, stack, cb, t_in, t_out, W)
        E.emit("y", lambda y: y.nop(), wait=[tlast])
        E.make_sems()
        with nc.Block() as block:
            E.materialize(block)
    return nc



class MambaBufs:
    def __init__(self, E, stack, tag="mb"):
        nc = E.nc
        self.xt = sbuf(nc, stack, tag + "_xt", [128, NDC, 512], F32)
        self.y0f = sbuf(nc, stack, tag + "_y0f", [128, NDC, 512], F32)
        self.y0b = sbuf(nc, stack, tag + "_y0b", [128, NDC, 512], BF16)
        self.stat = sbuf(nc, stack, tag + "_stat", [1, 6, 512], F32)
        self.xs = sbuf(nc, stack, tag + "_xs", [128, NDC, 515], F32)
        self.u = sbuf(nc, stack, tag + "_u", [128, NDC, 512], F32)
        self.ub = sbuf(nc, stack, tag + "_ub", [128, NDC, 512], BF16)
        self.sg = sbuf(nc, stack, tag + "_sg", [128, NDC, 512], F32)
        self.yb = sbuf(nc, stack, tag + "_yb", [128, NDC, 512], BF16)
        self.ytk = sbuf(nc, stack, tag + "_ytk", [128, 4, 32], F32)
        self.xsm = sbuf(nc, stack, tag + "_xsm", [32, 512], F32)
        self.tpl = sbuf(nc, stack, tag + "_tpl", [128, 32, 128], F32)
        self.qall = sbuf(nc, stack, tag + "_qall", [32, 129, 32], F32)
        self.mall = sbuf(nc, stack, tag + "_mall", [32, 128, 32], F32)
        self.qq = sbuf(nc, stack, tag + "_qq", [32, 129], F32)
        self.fr = sbuf(nc, stack, tag + "_fr", [32, 255], F32)
        self.rr = sbuf(nc, stack, tag + "_rr", [32, 33], F32)
        self.asb = sbuf(nc, stack, tag + "_asb", [32, 32], F32)
        self.i32 = sbuf(nc, stack, tag + "_i32", [32, 32], F32)
        self.i4 = sbuf(nc, stack, tag + "_i4", [4, 4], F32)
        self.bcol = sbuf(nc, stack, tag + "_bcol", [32, 2], F32)
        self.hsb = sbuf(nc, stack, tag + "_hsb", [32, 1], F32)
        self.hin = sbuf(nc, stack, tag + "_hin", [32, 4], F32)
        self.ccs = sbuf(nc, stack, tag + "_ccs", [4, 128], F32)
        self.crow = sbuf(nc, stack, tag + "_crow", [1, 512], F32)
        self.onr512 = sbuf(nc, stack, tag + "_onr512", [1, 512], F32)
        self.wsl = sbuf(nc, stack, tag + "_wsl", [128, 2, NDC, 128], BF16)
        self.inb = sbuf(nc, stack, tag + "_inb", [128, 16], F32)
        self.cwc = sbuf(nc, stack, tag + "_cwc", [128, NDC, 4], F32)
        self.cbc = sbuf(nc, stack, tag + "_cbc", [128, NDC], F32)
        self.ssw = sbuf(nc, stack, tag + "_ssw", [128, NDC, 32], F32)
        self.ssb = sbuf(nc, stack, tag + "_ssb", [1, 32], F32)
        self.dcol = sbuf(nc, stack, tag + "_dcol", [128, NDC], F32)
        self.ocol = sbuf(nc, stack, tag + "_ocol", [128, NDC], F32)
        self.tmp = sbuf(nc, stack, tag + "_tmp", [128, 512], F32)
        self.tmp2 = sbuf(nc, stack, tag + "_tmp2", [128, 512], F32)


def emit_mamba(E, cb, mb, pb, t_in, t_out, Wm, fr_dram, do_ln, tag):
    nc = E.nc
    TB = 512
    NTB = T // TB
    b_ev, b_ev2, b_sc, b_h, b_r, b_cb, ba, bb = pb

    tw = []
    tw.append(E.emit("y", lambda y: y.dma_start(out=mb.asb[:], in_=Wm["A"]), dma=True))
    tw.append(E.emit("y", lambda y: y.dma_start(out=mb.bcol[:, 0:1], in_=Wm["B_col"]), dma=True))
    tw.append(E.emit("y", lambda y: y.dma_start(out=mb.bcol[:, 1:2], in_=Wm["C_col"]), dma=True))
    tw.append(E.emit("y", lambda y: y.dma_start(out=mb.i32[:], in_=Wm["ident32"]), dma=True))
    tw.append(E.emit("y", lambda y: y.dma_start(out=mb.i4[:], in_=Wm["i4"]), dma=True))
    tw.append(E.emit("y", lambda y: y.dma_start(out=mb.onr512[:], in_=Wm["ones_row512"]), dma=True))
    tw.append(E.emit("y", lambda y: y.dma_start(out=mb.inb[:], in_=Wm["in_b_col"]), dma=True))
    tw.append(E.emit("y", lambda y: y.dma_start(out=mb.cwc[:], in_=Wm["conv_w_col"]), dma=True))
    tw.append(E.emit("y", lambda y: y.dma_start(out=mb.cbc[:], in_=Wm["conv_b_col"]), dma=True))
    tw.append(E.emit("y", lambda y: y.dma_start(
        out=mb.ssw[:], in_=Wm["ssm_w"].rearrange("(c p) s -> p c s", p=128)), dma=True))
    tw.append(E.emit("y", lambda y: y.dma_start(out=mb.ssb[:], in_=Wm["ssm_b_row"]), dma=True))
    tw.append(E.emit("y", lambda y: y.dma_start(out=mb.dcol[:], in_=Wm["D_col"]), dma=True))
    tw.append(E.emit("y", lambda y: y.dma_start(out=mb.ocol[:], in_=Wm["out_b_col"]), dma=True))
    twl = tw[-1]

    tini = E.emit("v", lambda v: (v.memset(mb.rr[:, :], 0.0),
                                  v.tensor_copy(mb.rr[:, 0:32], mb.i32[:]),
                                  v.tensor_copy(mb.rr[:, 32:33], mb.bcol[:, 1:2]))[-1],
                  wait=tw)
    tq0 = E.emit("v", lambda v: (v.tensor_copy(mb.qall[:, 0, :], mb.rr[:, 0:32]),
                                 v.tensor_copy(mb.qq[:, 0:1], mb.rr[:, 32:33]))[-1])
    prev = tq0
    for k in range(1, 129):
        tmm = E.emit("t", (lambda: lambda t: t.matmul(
            b_r[0:32, 0:33], mb.asb[:], mb.rr[:, :], start=True, stop=True))(),
            wait=[prev, tini])
        prev = E.emit("v", (lambda k=k: lambda v: (
            v.tensor_copy(mb.rr[:, :], b_r[0:32, 0:33]),
            v.tensor_copy(mb.qall[:, k, :], b_r[0:32, 0:32]),
            v.tensor_copy(mb.qq[:, k:k + 1], b_r[0:32, 32:33]))[-1])(), wait=[tmm])
    tchain = prev
    tmall = E.emit("v", lambda v: v.tensor_scalar(
        mb.mall[:, :, :], mb.qall[:, 0:128, :], mb.bcol[:, 0:1], None,
        op0=ALU.mult), wait=[tchain], force_wait=[tchain])
    tgt = E.emit("v", lambda v: v.tensor_scalar(
        mb.fr[:, 127:255], mb.qq[:, 0:128], mb.bcol[:, 0:1], None, op0=ALU.mult),
        force_wait=[tmall])
    tfr0 = E.emit("v", lambda v: v.memset(mb.fr[:, 0:127], 0.0))
    def mk_rev():
        def f(v):
            srcap = mb.fr[:, 127:255]
            rev = AP(srcap.tensor, srcap.offset + 127, [[srcap.ap[0][0], 32], [-1, 128]])
            return v.tensor_copy(mb.tmp[0:32, 0:128], rev)
        return f
    trev1 = E.emit("v", mk_rev(), force_wait=[tgt])
    trev2 = E.emit("v", lambda v: v.tensor_copy(mb.fr[:, 0:128], mb.tmp[0:32, 0:128]))
    tfr1 = E.emit("v", lambda v: v.memset(mb.fr[:, 128:255], 0.0))
    tfr = E.emit("y", lambda y: y.dma_start(out=fr_dram, in_=mb.fr[:, :]), dma=True,
                 wait=[trev2, tfr1, tfr0])
    ttp = None
    for s in range(32):
        def mk_tp(s=s):
            def f(y):
                srcap = AP(fr_dram.tensor, s * 255 + 127, [[1, 128], [-1, 128]])
                return y.dma_start(out=mb.tpl[:, s, :], in_=srcap,
                                   allow_slow_non_contiguous=True)
            return f
        ttp = E.emit("y", mk_tp(), dma=True, wait=[tfr] if s == 0 else ())
    thz = E.emit("v", lambda v: (v.memset(mb.xs[:, :, 0:3], 0.0),
                                 v.memset(mb.hsb[:, :], 0.0))[-1])

    prev_store = None
    prev_blk = {}
    for tb_ in range(NTB):
        txl = E.emit("y", (lambda tb_=tb_: lambda y: y.dma_start(
            out=mb.xt[:, :, :],
            in_=t_in[:, tb_ * TB:(tb_ + 1) * TB].rearrange("(c p) t -> p c t", p=128)))(),
            dma=True, wait=[prev_store] if prev_store is not None else ())
        if do_ln:
            y0_tok = emit_ln(E, cb, mb.xt[:, :, :], txl, mb.y0f, mb.y0f, mb.y0b,
                             mb.stat, ba, bb, TB,
                             extra_war=[prev_blk.get("oj"), twl])
        else:
            toks = []
            for dch in range(NDC):
                toks.append(E.emit("v", (lambda dch=dch: lambda v: (
                    v.tensor_copy(mb.y0f[:, dch, :], mb.xt[:, dch, :]),
                    v.tensor_copy(mb.y0b[:, dch, :], mb.xt[:, dch, :]))[-1])(),
                    wait=[txl, twl] if dch == 0 else ()))
            y0_tok = toks[-1]
        if tb_ == 0:
            thalo = E.emit("v", lambda v: v.memset(mb.xs[:, :, 0:3], 0.0),
                           wait=[thz] + ([prev_blk["u"]] if "u" in prev_blk else []))
        else:
            thalo = E.emit("v", lambda v: (
                [v.tensor_copy(mb.xs[:, dch, 0:3], mb.xs[:, dch, 512:515])
                 for dch in range(NDC)])[-1],
                wait=[thz] + ([prev_blk["u"]] if "u" in prev_blk else []))
        ip_toks = []
        for m in range(16):
            p = m % 2
            wld = [prev_blk.get(("ip", p))] if ("ip", p) in prev_blk else []
            twm = E.emit("g", (lambda m=m, p=p: lambda g: g.dma_start(
                out=mb.wsl[:, p, :, :],
                in_=Wm["in_w"][:, m * 128:(m + 1) * 128].rearrange(
                    "(c p) e -> p c e", p=128)))(), dma=True, wait=wld)
            def mk_ip(m=m, p=p):
                def f(t):
                    h = None
                    for dch in range(NDC):
                        h = t.matmul(pb[p][:, :], mb.wsl[:, p, dch, :],
                                     mb.y0b[:, dch, :], start=(dch == 0),
                                     stop=(dch == NDC - 1))
                    return h
                return f
            w = [twm, y0_tok]
            if m >= 2:
                w.append(ip_toks[m - 2])
            elif ("ipe", p) in prev_blk:
                w.append(prev_blk[("ipe", p)])
            tipm = E.emit("t", mk_ip(), wait=w)
            prev_blk[("ip", p)] = tipm
            if m < 8:
                tev = E.emit("v", (lambda m=m, p=p: lambda v: v.tensor_scalar(
                    mb.xs[:, m, 3:515], pb[p][:, :], mb.inb[:, m:m + 1], None,
                    op0=ALU.add))(), wait=[tipm, thalo],
                    force_wait=[ip_toks[-1]] if ip_toks else [])
            else:
                tev = E.emit("s", (lambda m=m, p=p: lambda s: s.activation(
                    out=mb.sg[:, m - 8, :], in_=pb[p][:, :], func=AF.Silu,
                    bias=mb.inb[:, m:m + 1]))(), wait=[tipm])
            ip_toks.append(tev)
            prev_blk[("ipe", p)] = tev
        t_xs = E.emit("v", lambda v: v.tensor_copy(mb.xs[:, 0, 0:1],
                                                   mb.xs[:, 0, 0:1]),
                      force_wait=ip_toks)
        conv_toks = []
        for dch in range(NDC):
            t1 = E.emit("v", (lambda dch=dch: lambda v: v.tensor_scalar(
                mb.tmp[:, :], mb.xs[:, dch, 0:512], mb.cwc[:, dch, 0:1], None,
                op0=ALU.mult))(), wait=[t_xs],
                force_wait=[conv_toks[-1]] if conv_toks else [])
            for j in range(1, 4):
                t1 = E.emit("v", (lambda dch=dch, j=j: lambda v:
                                  v.scalar_tensor_tensor(
                    mb.tmp[:, :], mb.xs[:, dch, j:512 + j], mb.cwc[:, dch, j:j + 1],
                    mb.tmp[:, :], op0=ALU.mult, op1=ALU.add))(), force_wait=[t1])
            tu = E.emit("s", (lambda dch=dch: lambda s: s.activation(
                out=mb.u[:, dch, :], in_=mb.tmp[:, :], func=AF.Silu,
                bias=mb.cbc[:, dch:dch + 1]))(), force_wait=[t1])
            tub = E.emit("v", (lambda dch=dch: lambda v: v.tensor_copy(
                mb.ub[:, dch, :], mb.u[:, dch, :]))(), wait=[tu])
            conv_toks.append(tub)
        t_u = conv_toks[-1]
        prev_blk["u"] = t_u
        tyt = []
        for c in range(4):
            def mk_xt(c=c):
                def f(t):
                    for dch in range(NDC):
                        t.matmul(b_sc[:, 0:32], mb.u[:, dch, c * 128:(c + 1) * 128],
                                 mb.ssw[:, dch, :], start=(dch == 0), stop=False)
                    return t.matmul(b_sc[:, 0:32], cb.oner[:], mb.ssb[:, :],
                                    start=False, stop=True)
                return f
            w = [t_u, twl]
            if c > 0:
                w.append(tyt[-1])
            elif "ytk" in prev_blk:
                w.append(prev_blk["ytk"])
            tm = E.emit("t", mk_xt(), wait=w)
            tyt.append(E.emit("v", (lambda c=c: lambda v: v.tensor_copy(
                mb.ytk[:, c, :], b_sc[:, 0:32]))(), wait=[tm]))
        prev_blk["ytk"] = tyt[-1]
        def mk_xf():
            def f(t):
                for dch in range(NDC):
                    t.matmul(b_h[0:32, :], mb.ssw[:, dch, :], mb.u[:, dch, :],
                             start=(dch == 0), stop=False)
                return t.matmul(b_h[0:32, :], mb.ssb[:, :],
                                mb.onr512[:, :], start=False, stop=True)
            return f
        txf = E.emit("t", mk_xf(), wait=[t_u] + ([prev_blk["xsm"]] if "xsm" in prev_blk else []))
        txsm = E.emit("v", lambda v: v.tensor_copy(mb.xsm[:, :], b_h[0:32, :]),
                      wait=[txf])
        prev_blk["xsm"] = txsm
        tlh = None
        for j in range(128):
            def mk_lh(j=j):
                def f(t):
                    xcols = AP(mb.xsm[:, :].tensor if False else mb.xsm[:, :].tensor,
                               j, [[mb.xsm[:, :].ap[0][0], 32], [128, 4]])
                    return t.matmul(b_h[0:32, 64:68], mb.mall[:, 127 - j, :],
                                    xcols, start=(j == 0), stop=(j == 127))
                return f
            tlh = E.emit("t", mk_lh(), wait=[txsm, tmall] if j == 0 else ())
        tlhv = E.emit("v", lambda v: v.tensor_copy(mb.tmp2[0:32, 0:4], b_h[0:32, 64:68]),
                      wait=[tlh])
        tst = tlhv
        for c in range(4):
            tcp = E.emit("v", (lambda c=c: lambda v: v.tensor_copy(
                mb.hin[:, c:c + 1], mb.hsb[:, :]))(), wait=[tst],
                force_wait=[tst])
            tmm = E.emit("t", (lambda c=c: lambda t: (
                t.matmul(b_r[0:32, 0:1], mb.qall[:, 128, :], mb.hsb[:, :],
                         start=True, stop=False),
                t.matmul(b_r[0:32, 0:1], mb.i32[:], mb.tmp2[0:32, c:c + 1],
                         start=False, stop=True))[-1])(), wait=[tcp, tlhv, tchain])
            tst = E.emit("v", (lambda: lambda v: v.tensor_copy(
                mb.hsb[:, :], b_r[0:32, 0:1]))(), wait=[tmm])
        t_hin = tst
        tsc = None
        for s in range(32):
            def mk_in(s=s):
                def f(t):
                    ya = mb.ytk[:, :, :]
                    lhs = AP(ya.tensor, ya.offset + s, [[ya.ap[0][0], 128], [32, 4]])
                    return t.matmul(b_sc[0:4, 0:128], lhs, mb.tpl[:, s, :],
                                    start=(s == 0), stop=False)
                return f
            w = [tyt[-1], ttp]
            if s == 0 and "crow" in prev_blk:
                w.append(prev_blk["crow"])
            tsc = E.emit("t", mk_in(), wait=w)
        tsc = E.emit("t", lambda t: t.matmul(
            b_sc[0:4, 0:128], mb.hin[:, :], mb.qq[:, 1:129], start=False, stop=True),
            wait=[t_hin])
        tccs = E.emit("v", lambda v: v.tensor_copy(mb.ccs[:, :], b_sc[0:4, 0:128]),
                      wait=[tsc])
        trw = None
        for c in range(4):
            trw = E.emit("t", (lambda c=c: lambda t: t.matmul(
                b_cb[0:1, c * 128:(c + 1) * 128], mb.i4[:, c:c + 1], mb.ccs[:, :],
                start=True, stop=True))(), wait=[tccs] if c == 0 else ())
        tcrow = E.emit("v", lambda v: v.tensor_copy(mb.crow[:, :], b_cb[0:1, 0:512]),
                       wait=[trw])
        prev_blk["crow"] = tcrow
        tcb = E.emit("t", lambda t: t.matmul(b_cb[:, :], cb.oner[:], mb.crow[:, :],
                                             start=True, stop=True), wait=[tcrow])
        ty = []
        for dch in range(NDC):
            tt1 = E.emit("v", (lambda dch=dch: lambda v: v.scalar_tensor_tensor(
                mb.tmp[:, :], b_cb[:, :], mb.dcol[:, dch:dch + 1], mb.u[:, dch, :],
                op0=ALU.add, op1=ALU.mult))(), wait=[tcb, t_u],
                force_wait=[ty[-1]] if ty else [])
            tt2 = E.emit("v", (lambda dch=dch: lambda v: v.tensor_tensor(
                mb.yb[:, dch, :], mb.tmp[:, :], mb.sg[:, dch, :], op=ALU.mult))(),
                force_wait=[tt1])
            ty.append(tt2)
        t_y = ty[-1]
        oj_toks = []
        for m in range(8):
            p = m % 2
            wld = [prev_blk.get(("oj", p))] if ("oj", p) in prev_blk else []
            twm = E.emit("g", (lambda m=m, p=p: lambda g: g.dma_start(
                out=mb.wsl[:, p, :, :],
                in_=Wm["out_w"][:, m * 128:(m + 1) * 128].rearrange(
                    "(c p) e -> p c e", p=128)))(), dma=True,
                wait=wld + [prev_blk[("ip", p)]])
            def mk_oj(m=m, p=p):
                def f(t):
                    h = None
                    for dch in range(NDC):
                        h = t.matmul(pb[p][:, :], mb.wsl[:, p, dch, :],
                                     mb.yb[:, dch, :], start=(dch == 0),
                                     stop=(dch == NDC - 1))
                    return h
                return f
            w = [twm, t_y]
            if m >= 2:
                w.append(oj_toks[m - 2])
            tojm = E.emit("t", mk_oj(), wait=w)
            prev_blk[("oj", p)] = tojm
            tev = E.emit("v", (lambda m=m, p=p: lambda v: v.scalar_tensor_tensor(
                mb.y0f[:, m, :], pb[p][:, :], mb.ocol[:, m:m + 1],
                mb.y0f[:, m, :], op0=ALU.add, op1=ALU.add))(), wait=[tojm],
                force_wait=[oj_toks[-1]] if oj_toks else [])
            oj_toks.append(tev)
        prev_blk["oj"] = oj_toks[-1]
        tmark = E.emit("v", lambda v: v.tensor_copy(mb.y0f[:, 0, 0:1], mb.y0f[:, 0, 0:1]),
                       force_wait=oj_toks)
        prev_store = E.emit("y", (lambda tb_=tb_: lambda y: y.dma_start(
            out=t_out[:, tb_ * TB:(tb_ + 1) * TB].rearrange("(c p) t -> p c t", p=128),
            in_=mb.y0f[:, :, :]))(), dma=True, wait=[tmark])
    return prev_store


def emit_l0a_conv(E, stack, cb, x_in, xm2_in, t_out, Wc, pb=None, tag="cv", barrier=None, dbg=None):
    nc = E.nc
    TB = 512
    NTB = T // TB
    BN = 1.0 / float(np.sqrt(1.0 + 1e-5))

    ot = sbuf(nc, stack, tag + "_ot", [128, NDC, 516], F32)
    ob = sbuf(nc, stack, tag + "_ob", [128, NDC, 516], BF16)
    occ = sbuf(nc, stack, tag + "_occ", [128, NDC, 514], BF16)
    xt = sbuf(nc, stack, tag + "_xt", [128, NDC, TB], F32)
    w1 = sbuf(nc, stack, tag + "_w1", [128, 3, NDC, 512], BF16)
    w4 = sbuf(nc, stack, tag + "_w4", [128, NDC, 512], BF16)
    w5 = sbuf(nc, stack, tag + "_w5", [128, 3, NDC, 1024], BF16)
    b1c = sbuf(nc, stack, tag + "_b1c", [128, 4], F32)
    tmp = sbuf(nc, stack, tag + "_tmp", [128, TB], F32)
    yac = sbuf(nc, stack, tag + "_yac", [128, NDC, TB], F32)

    bk = pb[0:4] if pb is not None else [
        psum(nc, stack, tag + f"_bk{i}", [128, TB]) for i in range(4)]

    bar = [barrier] if barrier is not None else []
    tw = []
    for k in range(3):
        tw.append(E.emit("g", (lambda k=k: lambda g: g.dma_start(
            out=w1[:, k, :, :], in_=Wc["w1T"][k].rearrange("(c p) o -> p c o", p=128)))(),
            dma=True, wait=bar if k == 0 else ()))
        tw.append(E.emit("g", (lambda k=k: lambda g: g.dma_start(
            out=w5[:, k, :, :], in_=Wc["w5T"][k].rearrange("(c p) o -> p c o", p=128)))(),
            dma=True))
    tw.append(E.emit("g", lambda g: g.dma_start(
        out=w4[:, :, :], in_=Wc["w4T"].rearrange("(c p) o -> p c o", p=128)), dma=True))
    tw.append(E.emit("y", lambda y: y.dma_start(out=b1c[:, :], in_=Wc["w1b_col"]),
                     dma=True, wait=bar))
    twl = tw[-2]
    twy = tw[-1]

    prev_store = None
    prev_oc = None
    prev_o1 = None
    for tb_ in range(NTB):
        lo = tb_ * TB - 2
        lo_c = max(lo, 0)
        hi_c = min(tb_ * TB + TB + 2, T)
        ncols = hi_c - lo_c
        off = lo_c - lo
        tz = E.emit("v", (lambda: lambda v: (v.memset(ot[:, :, 0:4], 0.0),
                                             v.memset(ot[:, :, 512:516], 0.0))[-1])(),
                    wait=[prev_o1] if prev_o1 is not None else bar)
        txm = E.emit("y", (lambda lo_c=lo_c, ncols=ncols, off=off: lambda y:
                           y.dma_start(
            out=ot[:, :, off:off + ncols],
            in_=xm2_in[:, lo_c:lo_c + ncols].rearrange("(c p) t -> p c t", p=128)))(),
            dma=True, wait=[tz, prev_store] if prev_store is not None else [tz])
        txl = E.emit("y", (lambda tb_=tb_: lambda y: y.dma_start(
            out=xt[:, :, :],
            in_=x_in[:, tb_ * TB:(tb_ + 1) * TB].rearrange("(c p) t -> p c t", p=128)))(),
            dma=True, wait=bar if tb_ == 0 else ())
        tcb = None
        for dch in range(NDC):
            tcb = E.emit("v", (lambda dch=dch: lambda v: v.tensor_copy(
                ob[:, dch, :], ot[:, dch, :]))(), wait=[txm] if dch == 0 else ())
        oc_toks = []
        for mt in range(4):
            def mk_c1(mt=mt):
                def f(t):
                    h = None
                    for k in range(3):
                        for dch in range(NDC):
                            h = t.matmul(
                                bk[mt % 2][:, 0:514] if False else bk[mt % 2][:, :],
                                w1[:, k, dch, mt * 128:(mt + 1) * 128],
                                ob[:, dch, k:k + 514][:, 0:512] if False
                                else ob[:, dch, k:k + 512],
                                start=(k == 0 and dch == 0),
                                stop=(k == 2 and dch == NDC - 1))
                    return h
                return f
            pass
        for half, (c0, cn) in enumerate([(0, 512), (512, 2)]):
            for mt in range(4):
                def mk_c1(mt=mt, c0=c0, cn=cn):
                    def f(t):
                        h = None
                        for k in range(3):
                            for dch in range(NDC):
                                h = t.matmul(
                                    bk[mt % 2][:, 0:cn],
                                    w1[:, k, dch, mt * 128:(mt + 1) * 128],
                                    ob[:, dch, c0 + k:c0 + k + cn],
                                    start=(k == 0 and dch == 0),
                                    stop=(k == 2 and dch == NDC - 1))
                        return h
                    return f
                w = [tcb, twl, twy]
                if oc_toks:
                    w.append(oc_toks[-2] if len(oc_toks) >= 2 else oc_toks[-1])
                if prev_oc is not None and not oc_toks:
                    w.append(prev_oc)
                tc1 = E.emit("t", mk_c1(), wait=w)
                te1 = E.emit("s", (lambda mt=mt, c0=c0, cn=cn: lambda s: s.activation(
                    out=occ[:, mt, c0:c0 + cn], in_=bk[mt % 2][:, 0:cn],
                    func=AF.Relu, bias=b1c[:, mt:mt + 1], scale=BN))(), wait=[tc1])
                oc_toks.append(te1)
                def mk_c4(mt=mt, c0=c0, cn=cn):
                    def f(t):
                        h = None
                        for dch in range(NDC):
                            h = t.matmul(bk[2 + mt % 2][:, 0:cn],
                                         w4[:, dch, mt * 128:(mt + 1) * 128],
                                         ob[:, dch, c0 + 1:c0 + 1 + cn],
                                         start=(dch == 0), stop=(dch == NDC - 1))
                        return h
                    return f
                tc4 = E.emit("t", mk_c4(), wait=w)
                te4 = E.emit("s", (lambda mt=mt, c0=c0, cn=cn: lambda s: s.activation(
                    out=occ[:, 4 + mt, c0:c0 + cn], in_=bk[2 + mt % 2][:, 0:cn],
                    func=AF.Relu))(), wait=[tc4])
                oc_toks.append(te4)
        t_oc = E.emit("s", lambda s: s.activation(out=occ[:, 0, 0:1], in_=occ[:, 0, 0:1],
                                                  func=AF.Copy), force_wait=oc_toks)
        if tb_ == 0:
            t_oc = E.emit("v", lambda v: v.memset(occ[:, :, 0:1], 0.0), wait=[t_oc])
        if tb_ == NTB - 1:
            t_oc = E.emit("v", lambda v: v.memset(occ[:, :, 513:514], 0.0), wait=[t_oc])
        prev_oc = t_oc
        if dbg is not None and tb_ == 0:
            E.emit("y", lambda y: y.dma_start(out=dbg["occ"], in_=occ[:, :, :]),
                   dma=True, wait=[t_oc])
            E.emit("y", lambda y: y.dma_start(out=dbg["ob"], in_=ob[:, :, :]),
                   dma=True, wait=[t_oc])
        o5_toks = []
        for mt in range(8):
            def mk_c5(mt=mt):
                def f(t):
                    h = None
                    for k in range(3):
                        for dch in range(NDC):
                            h = t.matmul(bk[mt % 2][:, :],
                                         w5[:, k, dch, mt * 128:(mt + 1) * 128],
                                         occ[:, dch, k:k + 512],
                                         start=(k == 0 and dch == 0),
                                         stop=(k == 2 and dch == NDC - 1))
                    return h
                return f
            w = [t_oc, twl]
            if mt >= 2:
                w.append(o5_toks[mt - 2][0])
            else:
                w.append(oc_toks[-1])
            tc5 = E.emit("t", mk_c5(), wait=w)
            te5 = E.emit("s", (lambda mt=mt: lambda s: s.activation(
                out=tmp[:, :], in_=bk[mt % 2][:, :], func=AF.Relu, scale=BN))(),
                wait=[tc5] + ([o5_toks[mt - 1][1]] if mt >= 1 else []))
            ta1 = E.emit("v", (lambda mt=mt: lambda v: v.tensor_tensor(
                yac[:, mt, :], tmp[:, :], ot[:, mt, 2:514], op=ALU.add))(),
                wait=[te5, txm])
            ta2 = E.emit("v", (lambda mt=mt: lambda v: v.tensor_tensor(
                yac[:, mt, :], yac[:, mt, :], xt[:, mt, :], op=ALU.add))(),
                wait=[txl])
            o5_toks.append((ta2, ta2))
        tmark = E.emit("v", lambda v: v.tensor_copy(yac[:, 0, 0:1], yac[:, 0, 0:1]),
                       force_wait=[t[0] for t in o5_toks])
        prev_o1 = tmark
        prev_store = E.emit("y", (lambda tb_=tb_: lambda y: y.dma_start(
            out=t_out[:, tb_ * TB:(tb_ + 1) * TB].rearrange("(c p) t -> p c t", p=128),
            in_=yac[:, :, :]))(), dma=True, wait=[tmark])
    return prev_store


def prep_l0_attn_a_weights(inputs):
    w = {}
    for pre in ["m1", "m2"]:
        w[pre + "_in_w"] = np.ascontiguousarray(inputs[f"L0_attn_a_{pre}_in_w"])
        w[pre + "_in_b_col"] = np.ascontiguousarray(
            inputs[f"L0_attn_a_{pre}_in_b"].reshape(16, 128).T)
        w[pre + "_conv_w_col"] = np.ascontiguousarray(
            inputs[f"L0_attn_a_{pre}_conv_w"].reshape(8, 128, 4).transpose(1, 0, 2))
        w[pre + "_conv_b_col"] = np.ascontiguousarray(
            inputs[f"L0_attn_a_{pre}_conv_b"].reshape(8, 128).T)
        w[pre + "_ssm_w"] = np.ascontiguousarray(inputs[f"L0_attn_a_{pre}_ssm_w"])
        w[pre + "_ssm_b_row"] = np.ascontiguousarray(
            inputs[f"L0_attn_a_{pre}_ssm_b"].reshape(1, 32))
        w[pre + "_A"] = np.ascontiguousarray(inputs[f"L0_attn_a_{pre}_A"])
        w[pre + "_B_col"] = np.ascontiguousarray(
            inputs[f"L0_attn_a_{pre}_B"].reshape(32, 1))
        w[pre + "_C_col"] = np.ascontiguousarray(
            inputs[f"L0_attn_a_{pre}_C"].reshape(32, 1))
        w[pre + "_D_col"] = np.ascontiguousarray(
            inputs[f"L0_attn_a_{pre}_D"].reshape(8, 128).T)
        w[pre + "_out_w"] = np.ascontiguousarray(inputs[f"L0_attn_a_{pre}_out_w"])
        w[pre + "_out_b_col"] = np.ascontiguousarray(
            inputs[f"L0_attn_a_{pre}_out_b"].reshape(8, 128).T)
    w["ident32"] = np.eye(32, dtype=np.float32)
    w["i4"] = np.eye(4, dtype=np.float32)
    w["ones_row512"] = np.ones((1, 512), np.float32)
    c1 = inputs["L0_attn_a_c1_w"]
    w["w1T"] = np.ascontiguousarray(c1.transpose(2, 1, 0))
    bn = 1.0 / np.float32(np.sqrt(1.0 + 1e-5))
    w["w1b_col"] = np.ascontiguousarray(
        (inputs["L0_attn_a_c1_b"] * bn).reshape(4, 128).T)
    w["w4T"] = np.ascontiguousarray(inputs["L0_attn_a_c4_w"][:, :, 0].T)
    w["w5T"] = np.ascontiguousarray(
        inputs["L0_attn_a_c5_w"].transpose(2, 1, 0))
    return w


def build_l0_attn_a_program(debug=False):
    from contextlib import ExitStack
    nc = bass.Bass("TRN2", target_bir_lowering=False)
    t_in = nc.dram_tensor("x_fm", [D, T], F32, kind="ExternalInput").ap()
    t_out = nc.dram_tensor("y_fm", [D, T], F32, kind="ExternalOutput").ap()
    W = {}
    shapes = dict(in_w=[D, 2048], in_b_col=[128, 16], conv_w_col=[128, 8, 4],
                  conv_b_col=[128, 8], ssm_w=[D, 32], ssm_b_row=[1, 32],
                  A=[32, 32], B_col=[32, 1], C_col=[32, 1], D_col=[128, 8],
                  out_w=[D, D], out_b_col=[128, 8])
    for pre in ["m1", "m2"]:
        for k, shp in shapes.items():
            W[pre + "_" + k] = nc.dram_tensor(pre + "_" + k, shp, F32,
                                              kind="ExternalInput").ap()
    for k, shp in [("ident32", [32, 32]), ("i4", [4, 4]), ("ones_row512", [1, 512]),
                   ("w1T", [3, D, 512]), ("w1b_col", [128, 4]), ("w4T", [D, 512]),
                   ("w5T", [3, D, D])]:
        W[k] = nc.dram_tensor(k, shp, F32, kind="ExternalInput").ap()
    kind = "ExternalOutput" if debug else "Internal"
    xm1 = nc.dram_tensor("xm1_scr", [D, T], F32, kind=kind).ap()
    xm2 = nc.dram_tensor("xm2_scr", [D, T], F32, kind=kind).ap()
    dbg = None
    if debug:
        dbg = {"occ": nc.dram_tensor("dbg_occ", [128, NDC, 514], BF16, kind="ExternalOutput").ap(),
               "ob": nc.dram_tensor("dbg_ob", [128, NDC, 516], BF16, kind="ExternalOutput").ap()}
    fr1 = nc.dram_tensor("fr1_scr", [32, 255], F32, kind="Internal").ap()
    fr2 = nc.dram_tensor("fr2_scr", [32, 255], F32, kind="Internal").ap()
    caps = declare_constants(nc)
    stack = ExitStack()
    with stack:
        E = Emitter(nc)
        cb = ConstBufs(E, stack, caps)
        mstack = ExitStack()
        mb = MambaBufs(E, mstack)
        pb = [psum(nc, stack, f"mb_pb{i}", [128, 512]) for i in range(8)]
        Wm1 = {k[3:]: v for k, v in W.items() if k.startswith("m1_")}
        Wm1.update({k: W[k] for k in ["ident32", "i4", "ones_row512"]})
        Wm2 = {k[3:]: v for k, v in W.items() if k.startswith("m2_")}
        Wm2.update({k: W[k] for k in ["ident32", "i4", "ones_row512"]})
        t1 = emit_mamba(E, cb, mb, pb, t_in, xm1, Wm1, fr1, True, "m1")
        t2 = emit_mamba(E, cb, mb, pb, xm1, xm2, Wm2, fr2, False, "m2")
        mstack.close()
        tc = emit_l0a_conv(E, stack, cb, t_in, xm2, t_out,
                           {k: W[k] for k in ["w1T", "w1b_col", "w4T", "w5T"]},
                           pb=pb, barrier=t2, dbg=dbg)
        E.emit("y", lambda y: y.nop(), wait=[tc])
        E.make_sems()
        with nc.Block() as block:
            E.materialize(block)
    return nc



def emit_l0_attn_b(E, stack, cb, t_in, t_out, W, scr, tag="m2", nc_limit=None, ntb_limit=None):
    from contextlib import ExitStack
    nc = E.nc
    TB = 512
    NTB = (T // TB) if ntb_limit is None else ntb_limit
    NC = 32 if nc_limit is None else nc_limit
    NH = 32
    NXC = 17

    xcv_d, zsg_d = scr

    dtk = sbuf(nc, stack, tag + "_dtk", [128, 32, 32], F32)
    atk = sbuf(nc, stack, tag + "_atk", [128, 32, 32], F32)
    cwc = sbuf(nc, stack, tag + "_cwc", [128, NXC, 4], F32)
    cbc = sbuf(nc, stack, tag + "_cbc", [128, NXC], F32)
    dbr = sbuf(nc, stack, tag + "_dbr", [1, 32], F32)
    arow = sbuf(nc, stack, tag + "_arow", [1, 32], F32)
    drow = sbuf(nc, stack, tag + "_drow", [1, 32], F32)
    abc = sbuf(nc, stack, tag + "_abc", [128, 32], F32)
    dcols = sbuf(nc, stack, tag + "_dcols", [128, 32], F32)
    nwc = sbuf(nc, stack, tag + "_nwc", [128, 16], F32)
    ltri = sbuf(nc, stack, tag + "_ltri", [128, 128], F32)
    e127 = sbuf(nc, stack, tag + "_e127", [128, 1], F32)
    mskl = sbuf(nc, stack, tag + "_mskl", [128, 128], F32)
    selh = sbuf(nc, stack, tag + "_selh", [32, 16, 128], F32)
    w2 = sbuf(nc, stack, tag + "_w2", [128, 16, 1024], BF16)
    thr = sbuf(nc, stack, tag + "_thr", [128, NDC, 1024], BF16)
    tmpv = sbuf(nc, stack, tag + "_tmpv", [128, 512], F32)
    ltk = sbuf(nc, stack, tag + "_ltk", [128, 32], F32)
    l127 = sbuf(nc, stack, tag + "_l127", [1, 32], F32)
    le32 = sbuf(nc, stack, tag + "_le32", [32, 1], F32)
    dke = sbuf(nc, stack, tag + "_dke", [128, 16], F32)
    nlr = sbuf(nc, stack, tag + "_nlr", [1, 128], F32)
    lrw = sbuf(nc, stack, tag + "_lrw", [1, 128], F32)
    elc = sbuf(nc, stack, tag + "_elc", [128, 1], F32)
    rst = sbuf(nc, stack, tag + "_rst", [128, 1], F32)
    racc = sbuf(nc, stack, tag + "_racc", [128, 4], F32)
    p1s = ExitStack()
    xt = sbuf(nc, p1s, tag + "_xt", [128, NDC, TB], F32)
    xnf = sbuf(nc, p1s, tag + "_xnf", [128, NDC, TB], F32)
    xnb = sbuf(nc, p1s, tag + "_xnb", [128, NDC, TB], BF16)
    stat = sbuf(nc, p1s, tag + "_stat", [1, 6, TB], F32)
    wsl = sbuf(nc, p1s, tag + "_wsl", [128, 2, NDC, 128], BF16)
    xbc = sbuf(nc, p1s, tag + "_xbc", [128, NXC, 515], F32)
    xcv = sbuf(nc, p1s, tag + "_xcv", [128, NXC, TB], BF16)
    zsg = sbuf(nc, p1s, tag + "_zsg", [128, 16, TB], BF16)

    pb = [psum(nc, stack, tag + f"_pb{i}", [128, 512]) for i in range(8)]
    ba, bb = pb[6], pb[7]

    tw = []
    for buf, key in [(cwc, "conv_w_col"), (cbc, "conv_b_col"), (dbr, "dt_bias_row"),
                     (drow, "D_row"), (nwc, "normw_col"), (ltri, "ltri"),
                     (e127, "e127"), (mskl, "maskl"), (selh, "selh")]:
        tw.append(E.emit("y", (lambda b=buf, k=key: lambda y: y.dma_start(
            out=b[:], in_=W[k]))(), dma=True))
    talog = E.emit("y", lambda y: y.dma_start(out=arow[:, :], in_=W["A_log_row"]),
                   dma=True)
    te1 = E.emit("s", lambda s: s.activation(out=arow[:, :], in_=arow[:, :],
                                             func=AF.Exp), wait=[talog])
    tneg = E.emit("v", lambda v: v.tensor_scalar(arow[:, :], arow[:, :], -1.0,
                                                 None, op0=ALU.mult), wait=[te1])
    tab = E.emit("t", lambda t: t.matmul(ba[:, 0:32], cb.oner[:], arow[:, :],
                                         start=True, stop=True),
                 wait=[tneg, cb.ready])
    tabc = E.emit("v", lambda v: v.tensor_copy(abc[:, :], ba[:, 0:32]), wait=[tab])
    tdb = E.emit("t", lambda t: t.matmul(ba[:, 32:64], cb.oner[:], drow[:, :],
                                         start=True, stop=True), wait=tw + [tabc])
    tdc = E.emit("v", lambda v: v.tensor_copy(dcols[:, :], ba[:, 32:64]), wait=[tdb])

    t2l = {0: None, 1: None}
    tthl = E.emit("g", lambda g: g.dma_start(
        out=thr[:, :, :], in_=W["to_hKT"].rearrange("(c p) e -> p c e", p=128)),
        dma=True)
    t2last = None
    for ft in range(16):
        p = ft % 2
        twa = E.emit("g", (lambda ft=ft, p=p: lambda g: g.dma_start(
            out=wsl[:, p, :, :],
            in_=W["out_proj"][:, ft * 128:(ft + 1) * 128].rearrange(
                "(c p) e -> p c e", p=128)))(), dma=True,
            wait=[t2l[p]] if t2l[p] is not None else [])
        for eh in range(2):
            def mk_w2(ft=ft, eh=eh, p=p):
                def f(t):
                    h = None
                    for dch in range(NDC):
                        h = t.matmul(pb[eh][:, :], wsl[:, p, dch, :],
                                     thr[:, dch, eh * 512:(eh + 1) * 512],
                                     start=(dch == 0), stop=(dch == NDC - 1))
                    return h
                return f
            w = [twa, tthl]
            if t2last is not None and eh == 0:
                w.append(t2last)
            tm = E.emit("t", mk_w2(), wait=w)
            t2l[p] = tm
            t2last = E.emit("v", (lambda ft=ft, eh=eh: lambda v: v.tensor_scalar(
                w2[:, ft, eh * 512:(eh + 1) * 512], pb[eh][:, :],
                nwc[:, ft:ft + 1], None, op0=ALU.mult))(), wait=[tm] + tw,
                force_wait=[t2last] if t2last is not None else [])
    tw2done = t2last

    prev = {}
    thz = E.emit("v", lambda v: v.memset(xbc[:, :, 0:3], 0.0))
    p1_done = None
    for tb_ in range(NTB):
        txl = E.emit("y", (lambda tb_=tb_: lambda y: y.dma_start(
            out=xt[:, :, :],
            in_=t_in[:, tb_ * TB:(tb_ + 1) * TB].rearrange("(c p) t -> p c t", p=128)))(),
            dma=True, wait=[prev["st"]] if "st" in prev else ())
        xn_tok = emit_ln(E, cb, xt[:, :, :], txl, xnf, xnf, xnb, stat, ba, bb,
                         TB, extra_war=[prev.get("cv"), tw2done])
        if tb_ == 0:
            thalo = E.emit("v", lambda v: v.memset(xbc[:, :, 0:3], 0.0),
                           wait=[thz] + ([prev["cv"]] if "cv" in prev else []))
        else:
            thalo = E.emit("v", lambda v: (
                [v.tensor_copy(xbc[:, ch, 0:3], xbc[:, ch, 512:515])
                 for ch in range(NXC)])[-1],
                wait=[prev["cv"]] if "cv" in prev else ())
        ip_toks = []
        for m in range(34):
            p = m % 2
            ncols = 128 if m < 33 else 32
            twm = E.emit("g", (lambda m=m, p=p, ncols=ncols: lambda g: g.dma_start(
                out=wsl[:, p, :, 0:ncols],
                in_=W["ipT"][:, m * 128:m * 128 + ncols].rearrange(
                    "(c p) e -> p c e", p=128)))(), dma=True,
                wait=[prev[("ip", p)]] if ("ip", p) in prev else
                     ([t2l[p]] if t2l[p] is not None else []))
            if m < 33:
                def mk_ip(m=m, p=p):
                    def f(t):
                        h = None
                        for dch in range(NDC):
                            h = t.matmul(pb[p][:, :], wsl[:, p, dch, 0:128],
                                         xnb[:, dch, :], start=(dch == 0),
                                         stop=(dch == NDC - 1))
                        return h
                    return f
            else:
                def mk_ip(m=m, p=p):
                    def f(t):
                        h = None
                        for st4 in range(4):
                            for dch in range(NDC):
                                h = t.matmul(
                                    pb[p][:, st4 * 32:(st4 + 1) * 32],
                                    xnb[:, dch, st4 * 128:(st4 + 1) * 128],
                                    wsl[:, p, dch, 0:32],
                                    start=(dch == 0), stop=False)
                            h = t.matmul(pb[p][:, st4 * 32:(st4 + 1) * 32],
                                         cb.oner[:], dbr[:, :],
                                         start=False, stop=True)
                        return h
                    return f
            w = [twm, xn_tok]
            if len(ip_toks) >= 2:
                w.append(ip_toks[-2])
            elif ("ipe", p) in prev:
                w.append(prev[("ipe", p)])
            tipm = E.emit("t", mk_ip(), wait=w)
            prev[("ip", p)] = tipm
            if m < 16:
                tev = E.emit("s", (lambda m=m, p=p: lambda s: s.activation(
                    out=zsg[:, m, :], in_=pb[p][:, :], func=AF.Silu))(),
                    wait=[tipm] + ([prev["zst"]] if (m == 0 and "zst" in prev) else []))
            elif m < 33:
                tev = E.emit("v", (lambda m=m, p=p: lambda v: v.tensor_copy(
                    xbc[:, m - 16, 3:515], pb[p][:, :]))(),
                    wait=[tipm, thalo])
            else:
                tspa = E.emit("s", (lambda p=p: lambda s: s.activation(
                    out=tmpv[:, 0:128], in_=pb[p][:, 0:128], func=AF.Abs))(),
                    wait=[tipm])
                tspb = E.emit("s", (lambda: lambda s: s.activation(
                    out=tmpv[:, 0:128], in_=tmpv[:, 0:128], func=AF.Exp,
                    scale=-1.0))(), force_wait=[tspa])
                tsp1 = E.emit("s", (lambda: lambda s: s.activation(
                    out=tmpv[:, 128:256], in_=tmpv[:, 0:128], func=AF.Ln,
                    bias=1.0))(), wait=[cb.ready], force_wait=[tspb])
                tsp2 = E.emit("v", (lambda p=p: lambda v: v.tensor_scalar(
                    tmpv[:, 256:384], pb[p][:, 0:128], 0.0, None,
                    op0=ALU.max))(), wait=[tsp1])
                tdta = E.emit("v", (lambda: lambda v: v.tensor_tensor(
                    tmpv[:, 256:384], tmpv[:, 256:384], tmpv[:, 128:256],
                    op=ALU.add))(), wait=[tsp1], force_wait=[tsp2])
                def mk_dt(tb_=tb_):
                    def f(v):
                        h = None
                        for st4 in range(4):
                            h = v.tensor_copy(
                                dtk[:, tb_ * 4 + st4, :],
                                tmpv[:, 256 + st4 * 32:256 + (st4 + 1) * 32])
                        return h
                    return f
                tdtb = E.emit("v", mk_dt(), force_wait=[tdta])
                def mk_at(tb_=tb_):
                    def f(v):
                        h = None
                        for st4 in range(4):
                            h = v.tensor_tensor(
                                atk[:, tb_ * 4 + st4, :],
                                dtk[:, tb_ * 4 + st4, :], abc[:, :], op=ALU.mult)
                        return h
                    return f
                tev = E.emit("v", mk_at(), wait=[tabc], force_wait=[tdtb])
            ip_toks.append(tev)
            prev[("ipe", p)] = tev
        tz_mark = E.emit("s", lambda s: s.activation(
            out=zsg[:, 0, 0:1], in_=zsg[:, 0, 0:1], func=AF.Copy),
            force_wait=[t for t in ip_toks[0:16]])
        prev["zst"] = E.emit("y", (lambda tb_=tb_: lambda y: y.dma_start(
            out=zsg_d[:, tb_ * TB:(tb_ + 1) * TB].rearrange("(c p) t -> p c t", p=128),
            in_=zsg[:, :, :]))(), dma=True, wait=[tz_mark])
        txbc = E.emit("v", lambda v: v.tensor_copy(xbc[:, 0, 0:1], xbc[:, 0, 0:1]),
                      force_wait=[t for t in ip_toks[16:33]])
        cv_toks = []
        for ch in range(NXC):
            t1 = E.emit("v", (lambda ch=ch: lambda v: v.tensor_scalar(
                tmpv[:, :], xbc[:, ch, 0:512], cwc[:, ch, 0:1], None,
                op0=ALU.mult))(),
                force_wait=[txbc] + ([cv_toks[-1]] if cv_toks else []))
            for j in range(1, 4):
                t1 = E.emit("v", (lambda ch=ch, j=j: lambda v:
                                  v.scalar_tensor_tensor(
                    tmpv[:, :], xbc[:, ch, j:512 + j], cwc[:, ch, j:j + 1],
                    tmpv[:, :], op0=ALU.mult, op1=ALU.add))(), force_wait=[t1])
            tu = E.emit("s", (lambda ch=ch: lambda s: s.activation(
                out=xcv[:, ch, :], in_=tmpv[:, :], func=AF.Silu,
                bias=cbc[:, ch:ch + 1]))(), force_wait=[t1],
                wait=[prev["cst"]] if (ch == 0 and "cst" in prev) else ())
            cv_toks.append(tu)
        prev["cv"] = E.emit("s", lambda s: s.activation(
            out=xcv[:, 0, 0:1], in_=xcv[:, 0, 0:1], func=AF.Copy),
            force_wait=cv_toks)
        prev["cst"] = E.emit("y", (lambda tb_=tb_: lambda y: y.dma_start(
            out=xcv_d[:, tb_ * TB:(tb_ + 1) * TB].rearrange("(c p) t -> p c t", p=128),
            in_=xcv[:, :, :]))(), dma=True, wait=[prev["cv"]])
        prev["st"] = prev["cst"]
        p1_done = prev["cst"]

    if NC == 0:
        p1s.close()
        return p1_done
    p1s.close()
    xtk = sbuf(nc, stack, tag + "_xtk", [128, 2, 2048], BF16)
    zk = sbuf(nc, stack, tag + "_zk", [128, 2, 2048], BF16)
    bfm = sbuf(nc, stack, tag + "_bfm", [64, 2, 128], BF16)
    cfm = sbuf(nc, stack, tag + "_cfm", [128, 2, 128], BF16)
    btk = sbuf(nc, stack, tag + "_btk", [128, 2, 64], BF16)
    esb = sbuf(nc, stack, tag + "_esb", [128, 128], F32)
    mpb = sbuf(nc, stack, tag + "_mpb", [128, 128], BF16)
    bw = sbuf(nc, stack, tag + "_bw", [128, 64], BF16)
    hst = sbuf(nc, stack, tag + "_hst", [128, 16, 64], F32)
    hstb = sbuf(nc, stack, tag + "_hstb", [128, 16, 64], BF16)
    yis = sbuf(nc, stack, tag + "_yis", [128, 64], F32)
    ytok = sbuf(nc, stack, tag + "_ytok", [128, 2048], F32)
    ynb = sbuf(nc, stack, tag + "_ynb", [128, 2048], BF16)
    yfm = sbuf(nc, stack, tag + "_yfm", [128, 16, 128], BF16)
    xr = sbuf(nc, stack, tag + "_xr", [128, NDC, 128], F32)
    yo = sbuf(nc, stack, tag + "_yo", [128, NDC, 128], F32)
    thst = E.emit("v", lambda v: (v.memset(hst[:, :, :], 0.0),
                                  v.memset(hstb[:, :, :], 0.0))[-1], wait=[p1_done])
    prev_c = {}
    last_store = None
    for c in range(NC):
        pc = c % 2
        tld = []
        tld.append(E.emit("y", (lambda c=c, pc=pc: lambda y: y.dma_start(
            out=xtk[:, pc, :], in_=xcv_d[0:2048, c * 128:(c + 1) * 128],
            transpose=True))(), dma=True,
            wait=[p1_done] + ([prev_c[("xu", pc)]] if ("xu", pc) in prev_c else [])))
        tld.append(E.emit("y", (lambda c=c, pc=pc: lambda y: y.dma_start(
            out=zk[:, pc, :], in_=zsg_d[:, c * 128:(c + 1) * 128],
            transpose=True))(), dma=True))
        tld.append(E.emit("y", (lambda c=c, pc=pc: lambda y: y.dma_start(
            out=btk[:, pc, :], in_=xcv_d[2048:2112, c * 128:(c + 1) * 128],
            transpose=True))(), dma=True))
        tld.append(E.emit("y", (lambda c=c, pc=pc: lambda y: y.dma_start(
            out=bfm[:, pc, :], in_=xcv_d[2048:2112, c * 128:(c + 1) * 128]))(),
            dma=True))
        tld.append(E.emit("y", (lambda c=c, pc=pc: lambda y: y.dma_start(
            out=cfm[0:64, pc, :], in_=xcv_d[2112:2176, c * 128:(c + 1) * 128]))(),
            dma=True))
        tld.append(E.emit("y", (lambda c=c, pc=pc: lambda y: y.dma_start(
            out=cfm[64:128, pc, :], in_=xcv_d[2112:2176, c * 128:(c + 1) * 128]))(),
            dma=True))
        txr_ = E.emit("y", (lambda c=c: lambda y: y.dma_start(
            out=xr[:, :, :],
            in_=t_in[:, c * 128:(c + 1) * 128].rearrange("(c2 p) t -> p c2 t", p=128)))(),
            dma=True, wait=[last_store] if last_store is not None else [p1_done])
        tldl = tld[-1]
        tl1 = E.emit("t", (lambda c=c: lambda t: t.matmul(
            ba[:, 0:32], ltri[:, :], atk[:, c, :], start=True, stop=True))(),
            wait=[p1_done] + ([prev_c.get("lend")] if prev_c.get("lend") else []))
        tl2 = E.emit("v", lambda v: v.tensor_copy(ltk[:, :], ba[:, 0:32]),
                     wait=[tl1])
        tl3 = E.emit("t", lambda t: t.matmul(ba[0:1, 32:64], e127[:, :],
                                             ltk[:, :], start=True, stop=True),
                     wait=[tl2])
        tl4 = E.emit("v", lambda v: v.tensor_copy(l127[:, :], ba[0:1, 32:64]),
                     wait=[tl3])
        tl5 = E.emit("t", lambda t: t.matmul(ba[0:32, 64:65], l127[:, :],
                                             cb.onec[0:1, :] if False else cb.oner[:, 0:1],
                                             start=True, stop=True), wait=[tl4])
        tl6 = E.emit("v", lambda v: v.tensor_copy(le32[:, :], ba[0:32, 64:65]),
                     wait=[tl5])
        tl7 = None
        for i in range(16):
            tl7 = E.emit("t", (lambda i=i: lambda t: t.matmul(
                ba[:, 80 + i:81 + i], selh[:, i, :], le32[:, :],
                start=True, stop=True))(), wait=[tl6] if i == 0 else ())
        tl8 = E.emit("s", lambda s: s.activation(out=dke[:, :], in_=ba[:, 80:96],
                                                 func=AF.Exp), wait=[tl7])
        prev_c["lend"] = tl8
        bg = pb[2]
        tgr = E.emit("t", (lambda pc=pc: lambda t: t.matmul(
            bg[:, 0:128], bfm[:, pc, :], cfm[0:64, pc, :], start=True, stop=True))(),
            wait=[tldl, tld[-2]] + ([prev_c.get("gram_use")] if prev_c.get("gram_use") else []))
        ysum_tok = None
        st_toks = []
        for h_ in range(NH):
            ti, lo = h_ // 2, (h_ % 2) * 64
            tn1 = E.emit("v", (lambda h_=h_: lambda v: v.tensor_scalar(
                tmpv[:, 0:1], ltk[:, h_:h_ + 1], -1.0, None, op0=ALU.mult))(),
                wait=[tl2], force_wait=[prev_c.get("eu")] if prev_c.get("eu") else [])
            tn2 = E.emit("t", (lambda: lambda t: t.matmul(
                bb[0:1, 0:128], tmpv[:, 0:1], cb.idf[:], start=True, stop=True))(),
                wait=[tn1] + ([prev_c["tn5"]] if prev_c.get("tn5") else []))
            tn3 = E.emit("v", lambda v: v.tensor_copy(nlr[:, :], bb[0:1, 0:128]),
                         wait=[tn2])
            tn4 = E.emit("t", (lambda h_=h_: lambda t: t.matmul(
                bb[0:1, 128:256], ltk[:, h_:h_ + 1], cb.idf[:],
                start=True, stop=True))(), wait=[tn1, tn3])
            tn5 = E.emit("v", lambda v: v.tensor_copy(lrw[:, :], bb[0:1, 128:256]),
                         wait=[tn4])
            prev_c["tn5"] = tn5
            bl = pb[3]
            tn6 = E.emit("t", (lambda: lambda t: (
                t.matmul(bl[:, 0:128], nlr[:, :], cb.oner[:], start=True, stop=False),
                t.matmul(bl[:, 0:128], cb.oner[:], lrw[:, :],
                         start=False, stop=True))[-1])(),
                wait=[tn3, tn5] + ([prev_c.get("eu")] if prev_c.get("eu") else []))
            tn7 = E.emit("v", (lambda: lambda v: v.tensor_tensor(
                bl[:, 0:128], bl[:, 0:128], mskl[:, :], op=ALU.add))(), wait=[tn6])
            tn8 = E.emit("s", (lambda: lambda s: s.activation(
                out=esb[:, :], in_=bl[:, 0:128], func=AF.Exp))(), wait=[tn7])
            prev_c["eu"] = tn8
            tm1 = E.emit("v", (lambda c=c, h_=h_: lambda v: v.scalar_tensor_tensor(
                mpb[:, :], bg[:, 0:128], dtk[:, c, h_:h_ + 1], esb[:, :],
                op0=ALU.mult, op1=ALU.mult))(), wait=[tgr, tn8],
                force_wait=[tn8])
            by, bc_ = pb[4], pb[5]
            wym = [tm1, tld[0]]
            if prev_c.get("ty1") is not None:
                wym.append(prev_c["ty1"])
            tym = E.emit("t", (lambda pc=pc, h_=h_: lambda t: t.matmul(
                by[:, 0:64], mpb[:, :], xtk[:, pc, h_ * 64:(h_ + 1) * 64],
                start=True, stop=True))(), wait=wym)
            wyc = [thst] + st_toks[-1:]
            if prev_c.get("ty2") is not None:
                wyc.append(prev_c["ty2"])
            tyc = E.emit("t", (lambda pc=pc, h_=h_, ti=ti, lo=lo: lambda t: t.matmul(
                bc_[:, 0:64], cfm[lo:lo + 64, pc, :], hstb[lo:lo + 64, ti, :],
                start=True, stop=True))(), wait=wyc)
            tel = E.emit("s", (lambda h_=h_: lambda s: s.activation(
                out=elc[:, :], in_=ltk[:, h_:h_ + 1], func=AF.Exp))(), wait=[tl2])
            ty1 = E.emit("v", (lambda: lambda v: v.tensor_copy(yis[:, :],
                                                               by[:, 0:64]))(),
                         wait=[tym])
            ty2 = E.emit("v", (lambda h_=h_: lambda v: v.scalar_tensor_tensor(
                yis[:, :], bc_[:, 0:64], elc[:, :], yis[:, :],
                op0=ALU.mult, op1=ALU.add))(), wait=[tyc, tel], force_wait=[ty1])
            prev_c["ty1"], prev_c["ty2"] = ty1, ty2
            ty3 = E.emit("v", (lambda pc=pc, h_=h_: lambda v: v.scalar_tensor_tensor(
                ytok[:, h_ * 64:(h_ + 1) * 64], xtk[:, pc, h_ * 64:(h_ + 1) * 64],
                dcols[:, h_:h_ + 1], yis[:, :], op0=ALU.mult, op1=ALU.add))(),
                wait=[tdc], force_wait=[ty2])
            ysum_tok = ty3
            tb1 = E.emit("v", (lambda pc=pc, c=c, h_=h_: lambda v: v.tensor_scalar(
                bw[:, :], btk[:, pc, :], dtk[:, c, h_:h_ + 1],
                esb[:, 127:128], op0=ALU.mult, op1=ALU.mult))(),
                wait=[tld[2], tn8], force_wait=[tm1])
            bh = pb[1]
            wb2 = [tb1] + st_toks[-1:]
            if h_ <= 1 and prev_c.get("pb1use") is not None:
                wb2.append(prev_c["pb1use"])
            tb2 = E.emit("t", (lambda pc=pc, lo=lo, h_=h_: lambda t: t.matmul(
                bh[lo:lo + 64, 0:64], bw[:, :],
                xtk[:, pc, h_ * 64:(h_ + 1) * 64],
                start=True, stop=True))(), wait=wb2)
            if h_ % 2 == 1:
                tst = E.emit("v", (lambda ti=ti: lambda v: v.scalar_tensor_tensor(
                    hst[:, ti, :], hst[:, ti, :], dke[:, ti:ti + 1], bh[:, 0:64],
                    op0=ALU.mult, op1=ALU.add))(), wait=[tb2, tl8, thst],
                    force_wait=[tb1])
                tst = E.emit("v", (lambda ti=ti: lambda v: v.tensor_copy(
                    hstb[:, ti, :], hst[:, ti, :]))(), force_wait=[tst])
                st_toks.append(tst)
        tq1 = E.emit("v", (lambda pc=pc: lambda v: v.tensor_tensor(
            ytok[:, :], ytok[:, :], zk[:, pc, :], op=ALU.mult))(),
            wait=[ysum_tok, tld[1]], force_wait=[ysum_tok])
        tsq = []
        for q in range(4):
            tsq.append(E.emit("s", (lambda q=q, racc=racc: lambda s: s.activation(
                out=tmpv[:, :], in_=ytok[:, q * 512:(q + 1) * 512], func=AF.Square,
                accum_out=racc[:, q:q + 1]))(), wait=[tq1] if q == 0 else (),
                force_wait=[tsq[-1]] if tsq else ()))
        trs = E.emit("v", (lambda racc=racc: lambda v: v.tensor_reduce(
            rst[:, :], racc[:, :], op=ALU.add, axis=AX.X))(), wait=[tsq[-1]],
            force_wait=[tsq[-1]])
        trs2 = E.emit("v", lambda v: v.tensor_scalar(rst[:, :], rst[:, :],
                                                     1.0 / 2048.0, 1e-5,
                                                     op0=ALU.mult, op1=ALU.add),
                      force_wait=[trs])
        trs3 = E.emit("s", lambda s: s.activation(out=rst[:, :], in_=rst[:, :],
                                                  func=AF.Ln), wait=[trs2])
        trs4 = E.emit("s", lambda s: s.activation(out=rst[:, :], in_=rst[:, :],
                                                  func=AF.Exp, scale=-0.5),
                      force_wait=[trs3])
        tyn = E.emit("v", lambda v: v.tensor_scalar(ynb[:, :], ytok[:, :],
                                                    rst[:, :], None, op0=ALU.mult),
                     wait=[trs4], force_wait=[trs4])
        ttr = []
        for i in range(16):
            bt_ = pb[0] if i % 2 == 0 else pb[4]
            t1 = E.emit("t", (lambda i=i, bt_=bt_: lambda t: t.transpose(
                bt_.bitcast(BF16)[:, 0:128],
                ynb[:, i * 128:(i + 1) * 128], cb.idb[:]))(),
                wait=[tyn, cb.ready] + ([ttr[i - 2]] if i >= 2 else
                                        st_toks[-1:] + ([prev_c.get("ty1")] if prev_c.get("ty1") else [])))
            t2 = E.emit("v", (lambda i=i, bt_=bt_: lambda v: v.tensor_copy(
                yfm[:, i, :], bt_.bitcast(BF16)[:, 0:128]))(),
                wait=[t1])
            ttr.append(t2)
        ty_toks = []
        for mt in range(8):
            def mk_o(mt=mt):
                def f(t):
                    h = None
                    for ft in range(16):
                        h = t.matmul(pb[1][:, 0:128],
                                     w2[:, ft, mt * 128:(mt + 1) * 128],
                                     yfm[:, ft, :], start=(ft == 0),
                                     stop=(ft == 15))
                    return h
                return f
            w = [ttr[-1], tw2done]
            if ty_toks:
                w.append(ty_toks[-1])
            else:
                w.append(st_toks[-1])
            tom = E.emit("t", mk_o(), wait=w)
            ty_toks.append(E.emit("v", (lambda mt=mt: lambda v: v.tensor_tensor(
                yo[:, mt, :], pb[1][:, 0:128], xr[:, mt, :], op=ALU.add))(),
                wait=[tom, txr_]))
        tmk = E.emit("v", lambda v: v.tensor_copy(yo[:, 0, 0:1], yo[:, 0, 0:1]),
                     force_wait=ty_toks)
        last_store = E.emit("y", (lambda c=c: lambda y: y.dma_start(
            out=t_out[:, c * 128:(c + 1) * 128].rearrange("(c2 p) t -> p c2 t", p=128),
            in_=yo[:, :, :]))(), dma=True, wait=[tmk])
        prev_c[("xu", pc)] = ty_toks[-1]
        prev_c["pb1use"] = ty_toks[-1]
        prev_c["gram_use"] = tm1
    return last_store


def prep_l0_attn_b_weights(inputs):
    w = {}
    w["ipT"] = np.ascontiguousarray(inputs["L0_attn_b_in_proj_w"].T)
    w["conv_w_col"] = np.zeros((128, 17, 4), np.float32)
    cw = inputs["L0_attn_b_conv_w"][:, 0, :]
    w["conv_w_col"][:, :, :] = cw.reshape(17, 128, 4).transpose(1, 0, 2)
    w["conv_b_col"] = np.ascontiguousarray(
        inputs["L0_attn_b_conv_b"].reshape(17, 128).T)
    w["dt_bias_row"] = np.ascontiguousarray(inputs["L0_attn_b_dt_bias"].reshape(1, 32))
    w["A_log_row"] = np.ascontiguousarray(inputs["L0_attn_b_A_log"].reshape(1, 32))
    w["D_row"] = np.ascontiguousarray(inputs["L0_attn_b_D"].reshape(1, 32))
    w["normw_col"] = np.ascontiguousarray(
        inputs["L0_attn_b_norm_w"].reshape(16, 128).T)
    w["out_proj"] = np.ascontiguousarray(inputs["L0_attn_b_out_proj_w"])
    w["to_hKT"] = np.ascontiguousarray(inputs["L0_attn_b_to_h_w"][:1024].T)
    tt = np.arange(128)
    w["ltri"] = (tt[:, None] <= tt[None, :]).astype(np.float32)
    w["e127"] = (tt == 127).astype(np.float32).reshape(128, 1)
    w["maskl"] = np.where(tt[:, None] <= tt[None, :], 0.0, -1e30).astype(np.float32)
    sel = np.zeros((32, 16, 128), np.float32)
    for i in range(16):
        for p in range(128):
            sel[2 * i + p // 64, i, p] = 1.0
    w["selh"] = sel
    return w


def build_l0_attn_b_program(nc_limit=None, ntb_limit=None):
    from contextlib import ExitStack
    nc = bass.Bass("TRN2", target_bir_lowering=False)
    t_in = nc.dram_tensor("x_fm", [D, T], F32, kind="ExternalInput").ap()
    t_out = nc.dram_tensor("y_fm", [D, T], F32, kind="ExternalOutput").ap()
    W = {}
    for k, shp in [("ipT", [D, 4256]), ("conv_w_col", [128, 17, 4]),
                   ("conv_b_col", [128, 17]), ("dt_bias_row", [1, 32]),
                   ("A_log_row", [1, 32]), ("D_row", [1, 32]),
                   ("normw_col", [128, 16]), ("out_proj", [D, 2048]),
                   ("to_hKT", [D, D]), ("ltri", [128, 128]), ("e127", [128, 1]),
                   ("maskl", [128, 128]), ("selh", [32, 16, 128])]:
        W[k] = nc.dram_tensor(k, shp, F32, kind="ExternalInput").ap()
    xcv_d = nc.dram_tensor("xcv_scr", [2176, T], BF16, kind="Internal").ap()
    zsg_d = nc.dram_tensor("zsg_scr", [2048, T], BF16, kind="Internal").ap()
    caps = declare_constants(nc)
    stack = ExitStack()
    with stack:
        E = Emitter(nc)
        cb = ConstBufs(E, stack, caps)
        tlast = emit_l0_attn_b(E, stack, cb, t_in, t_out, W, (xcv_d, zsg_d), nc_limit=nc_limit, ntb_limit=ntb_limit)
        E.emit("y", lambda y: y.nop(), wait=[tlast])
        E.make_sems()
        with nc.Block() as block:
            E.materialize(block)
    return nc



def emit_l1_attn_a(E, stack, cb, t_in, t_out, W, scr, tag="da", dbg=None):
    from contextlib import ExitStack
    nc = E.nc
    TB = 512
    NTB = T // TB
    NC = 32
    H = 4
    q_d, k_d, v_d, g_d = scr

    nwc = sbuf(nc, stack, tag + "_nwc", [128, 8], F32)
    mls = sbuf(nc, stack, tag + "_mls", [128, 128], BF16)
    mlq = sbuf(nc, stack, tag + "_mlq", [128, 128], BF16)
    beta = sbuf(nc, stack, tag + "_beta", [128, 32, H], F32)
    ssb = sbuf(nc, stack, tag + "_ssb", [128, H, 256], F32)
    ssbb = sbuf(nc, stack, tag + "_ssbb", [128, H, 256], BF16)
    wonb = sbuf(nc, stack, tag + "_wonb", [128, NDC, 1024], BF16)
    tw = [E.emit("y", lambda y: y.dma_start(out=nwc[:], in_=W["normw_col"]), dma=True),
          E.emit("y", lambda y: y.dma_start(out=mls[:], in_=W["mtril_s"]), dma=True),
          E.emit("y", lambda y: y.dma_start(out=mlq[:], in_=W["mtril_i"]), dma=True)]
    pb = [psum(nc, stack, tag + f"_pb{i}", [128, 512]) for i in range(8)]
    ba, bb = pb[6], pb[7]

    two_ = E.emit("g", lambda g: g.dma_start(
        out=wonb[:, :, :], in_=W["wo"].rearrange("(c p) e -> p c e", p=128)),
        dma=True)
    tw2 = []
    for ch in range(NDC):
        tw2.append(E.emit("v", (lambda ch=ch: lambda v: v.tensor_scalar(
            wonb[:, ch, :], wonb[:, ch, :], nwc[:, ch:ch + 1], None,
            op0=ALU.mult))(), wait=[two_] + tw,
            force_wait=[tw2[-1]] if tw2 else []))
    twon = tw2[-1]

    p1s = ExitStack()
    xt = sbuf(nc, p1s, tag + "_xt", [128, NDC, TB], F32)
    xnf = sbuf(nc, p1s, tag + "_xnf", [128, NDC, TB], F32)
    xnb = sbuf(nc, p1s, tag + "_xnb", [128, NDC, TB], BF16)
    stat = sbuf(nc, p1s, tag + "_stat", [1, 6, TB], F32)
    wsl = sbuf(nc, p1s, tag + "_wsl", [128, 2, NDC, 128], BF16)
    wmv = sbuf(nc, p1s, tag + "_wmv", [128, 2, TB], BF16)
    prj = sbuf(nc, p1s, tag + "_prj", [128, TB], BF16)
    pr4 = sbuf(nc, p1s, tag + "_pr4", [128, 4, 1024], BF16)
    sqs = sbuf(nc, p1s, tag + "_sqs", [128, TB], F32)
    nrow = sbuf(nc, p1s, tag + "_nrow", [1, TB], F32)

    prev = {}
    p1_done = None
    for tb_ in range(NTB):
        txl = E.emit("y", (lambda tb_=tb_: lambda y: y.dma_start(
            out=xt[:, :, :],
            in_=t_in[:, tb_ * TB:(tb_ + 1) * TB].rearrange("(c p) t -> p c t", p=128)))(),
            dma=True, wait=[prev["st"]] if "st" in prev else tw)
        xn_tok = emit_ln(E, cb, xt[:, :, :], txl, xnf, xnf, xnb, stat, ba, bb,
                         TB, extra_war=[prev.get("vg_v"), prev.get("vg_s"),
                                        prev.get("evb")])
        for wkey, dst, scale, norm in [("wq", q_d, 0.08838834764831845, False),
                                       ("wk", k_d, 1.0, True)]:
            for mt in range(H):
                p = mt % 2
                twm = E.emit("g", (lambda wkey=wkey, mt=mt, p=p: lambda g: g.dma_start(
                    out=wsl[:, p, :, :],
                    in_=W[wkey][:, mt * 128:(mt + 1) * 128].rearrange(
                        "(c p) e -> p c e", p=128)))(), dma=True,
                    wait=[prev[("w", p)]] if ("w", p) in prev else [])
                def mk_m(mt=mt, p=p):
                    def f(t):
                        h = None
                        for dch in range(NDC):
                            h = t.matmul(pb[p][:, :], wsl[:, p, dch, :],
                                         xnb[:, dch, :], start=(dch == 0),
                                         stop=(dch == NDC - 1))
                        return h
                    return f
                w = [twm, xn_tok]
                if ("ev", p) in prev:
                    w.append(prev[("ev", p)])
                if wkey == "wq" and mt < 2:
                    w += [x for x in (prev.get("vg_v"), prev.get("vg_s")) if x]
                tm = E.emit("t", mk_m(), wait=w)
                prev[("w", p)] = tm
                if not norm:
                    tev = E.emit("s", (lambda p=p, scale=scale: lambda s: s.activation(
                        out=prj[:, :], in_=pb[p][:, :], func=AF.Copy, scale=scale))(),
                        wait=[tm] + ([prev["st"]] if "st" in prev else []))
                else:
                    ts1 = E.emit("s", (lambda p=p: lambda s: s.activation(
                        out=sqs[:, :], in_=pb[p][:, :], func=AF.Square))(),
                        wait=[tm] + ([prev["st"]] if "st" in prev else []))
                    ts2 = E.emit("t", (lambda: lambda t: t.matmul(
                        bb[0:1, :], cb.onec[:], sqs[:, :], start=True, stop=True))(),
                        wait=[ts1] + ([prev["bbu"]] if "bbu" in prev else []))
                    ts3 = E.emit("v", (lambda: lambda v: v.tensor_scalar(
                        nrow[:, :], bb[0:1, :], 1e-12, None, op0=ALU.add))(),
                        wait=[ts2])
                    ts4 = E.emit("s", (lambda: lambda s: s.activation(
                        out=nrow[:, :], in_=nrow[:, :], func=AF.Ln))(), wait=[ts3])
                    ts5 = E.emit("s", (lambda: lambda s: s.activation(
                        out=nrow[:, :], in_=nrow[:, :], func=AF.Exp, scale=-0.5))(),
                        force_wait=[ts4])
                    ts6 = E.emit("t", (lambda: lambda t: t.matmul(
                        bb[:, :], cb.oner[:], nrow[:, :], start=True, stop=True))(),
                        wait=[ts5, ts3])
                    ts7 = E.emit("v", (lambda: lambda v: v.tensor_copy(
                        sqs[:, :], bb[:, :]))(), wait=[ts6])
                    tev = E.emit("v", (lambda p=p: lambda v: v.tensor_tensor(
                        prj[:, :], pb[p][:, :], sqs[:, :], op=ALU.mult))(),
                        force_wait=[ts7])
                    prev["bbu"] = ts7
                prev[("ev", p)] = tev
                tst = E.emit("y", (lambda dst=dst, mt=mt, tb_=tb_: lambda y: y.dma_start(
                    out=dst[mt * 128:(mt + 1) * 128, tb_ * TB:(tb_ + 1) * TB],
                    in_=prj[:, :]))(), dma=True, wait=[tev])
                prev["st"] = tst
        twb = E.emit("g", (lambda: lambda g: g.dma_start(
            out=wsl[:, 0, :, 0:4],
            in_=W["wb"].rearrange("(c p) e -> p c e", p=128)))(), dma=True,
            wait=[prev[("w", 0)]] if ("w", 0) in prev else [])
        def mk_bt():
            def f(t):
                h = None
                for st4 in range(4):
                    for dch in range(NDC):
                        h = t.matmul(ba[:, st4 * 4:st4 * 4 + 4],
                                     xnb[:, dch, st4 * 128:(st4 + 1) * 128],
                                     wsl[:, 0, dch, 0:4],
                                     start=(dch == 0), stop=(dch == NDC - 1))
                return h
            return f
        tbt = E.emit("t", mk_bt(), wait=[twb, xn_tok] +
                     ([prev[("ev", 0)]] if ("ev", 0) in prev else []))
        prev[("w", 0)] = tbt
        tbs = E.emit("s", (lambda tb_=tb_: lambda s: s.activation(
            out=beta[:, tb_ * 4:(tb_ + 1) * 4, :].rearrange("p a b -> p (a b)")
            if False else beta[:, tb_ * 4, :],
            in_=ba[:, 0:4], func=AF.Sigmoid))(), wait=[tbt]) if False else None
        tbs = None
        for st4 in range(4):
            tbs = E.emit("s", (lambda st4=st4, tb_=tb_: lambda s: s.activation(
                out=beta[:, tb_ * 4 + st4, :], in_=ba[:, st4 * 4:st4 * 4 + 4],
                func=AF.Sigmoid))(), wait=[tbt] if st4 == 0 else (),
                force_wait=[tbs] if tbs else [])
        prev[("evb")] = tbs
        ev2 = None
        for wkey, dst, act in [("wv", v_d, None), ("wg", g_d, AF.Silu)]:
            for half in range(2):
                pm = half % 2
                tvms = [None] * 4
                for dch in range(NDC):
                    twv = E.emit("g", (lambda wkey=wkey, half=half, dch=dch, pm=pm:
                                       lambda g: g.dma_start(
                        out=wmv[:, pm, :],
                        in_=W[wkey][dch * 128:(dch + 1) * 128,
                                    half * 512:(half + 1) * 512]))(), dma=True,
                        wait=[prev[("wv", pm)]] if ("wv", pm) in prev else [])
                    for st4 in range(4):
                        w = [twv, xn_tok]
                        if dch == 0 and st4 == 0:
                            if ev2 is not None:
                                w.append(ev2)
                            elif prev.get("ev2v") is not None:
                                w.append(prev["ev2v"])
                            w += [x for x in (prev.get(("ev", 0)), prev.get(("ev", 1)),
                                              prev.get("evb")) if x]
                        tvm = E.emit("t", (lambda st4=st4, dch=dch, pm=pm,
                                           bank=pb[st4 % 4]: lambda t: t.matmul(
                            bank[:, :], xnb[:, dch, st4 * 128:(st4 + 1) * 128],
                            wmv[:, pm, :], start=(dch == 0), stop=(dch == NDC - 1)))(),
                            wait=w)
                        tvms[st4] = tvm
                    prev[("wv", pm)] = tvms[3]
                for st4 in range(4):
                    if act is None:
                        ev2 = E.emit("v", (lambda st4=st4, half=half, bank=pb[st4 % 4]:
                                           lambda v: v.tensor_copy(
                            pr4[:, st4, half * 512:(half + 1) * 512] if False else
                            pr4[:, st4, half * 512:(half + 1) * 512], bank[:, :]))(),
                            wait=[tvms[st4]])
                    else:
                        ev2 = E.emit("s", (lambda st4=st4, half=half, bank=pb[st4 % 4]:
                                           lambda s: s.activation(
                            out=pr4[:, st4, half * 512:(half + 1) * 512],
                            in_=bank[:, :], func=act))(), wait=[tvms[st4]])
            for st4 in range(4):
                tst = E.emit("y", (lambda dst=dst, st4=st4, tb_=tb_: lambda y:
                                   y.dma_start(
                    out=dst[tb_ * TB + st4 * 128:tb_ * TB + (st4 + 1) * 128, :],
                    in_=pr4[:, st4, :]))(), dma=True,
                    wait=[ev2] if st4 == 0 else ())
                prev["st"] = tst
            prev["ev2v"] = ev2
            if wkey == "wv":
                prev["vg_v"] = ev2
            else:
                prev["vg_s"] = ev2
        p1_done = prev["st"]
    p1s.close()

    qf = sbuf(nc, stack, tag + "_qf", [128, 2, H, 128], BF16)
    kf = sbuf(nc, stack, tag + "_kf", [128, 2, H, 128], BF16)
    kt = sbuf(nc, stack, tag + "_kt", [128, 2, 512], BF16)
    vt = sbuf(nc, stack, tag + "_vt", [128, 2, 1024], BF16)
    gt = sbuf(nc, stack, tag + "_gt", [128, 2, 1024], BF16)
    xr = sbuf(nc, stack, tag + "_xr", [128, NDC, 128], F32)
    yo = sbuf(nc, stack, tag + "_yo", [128, NDC, 128], F32)
    nn_ = sbuf(nc, stack, tag + "_nn", [128, 128], BF16)
    nnt = sbuf(nc, stack, tag + "_nnt", [128, 128], BF16)
    rr_ = sbuf(nc, stack, tag + "_rr", [128, 128], BF16)
    rrt = sbuf(nc, stack, tag + "_rrt", [128, 128], BF16)
    pw = sbuf(nc, stack, tag + "_pw", [128, 128], BF16)
    pwt = sbuf(nc, stack, tag + "_pwt", [128, 128], BF16)
    w0b = sbuf(nc, stack, tag + "_w0b", [128, 256], BF16)
    ub = sbuf(nc, stack, tag + "_ub", [128, 256], BF16)
    gkq = sbuf(nc, stack, tag + "_gkq", [128, 128], BF16)
    otok = sbuf(nc, stack, tag + "_otok", [128, H, 256], F32)
    onb = sbuf(nc, stack, tag + "_onb", [128, 1024], BF16)
    ofm = sbuf(nc, stack, tag + "_ofm", [128, 8, 128], BF16)
    rst = sbuf(nc, stack, tag + "_rst", [128, 1], F32)
    tmpo = sbuf(nc, stack, tag + "_tmpo", [128, 256], F32)

    tz = E.emit("v", lambda v: (v.memset(ssb[:, :, :], 0.0),
                                v.memset(ssbb[:, :, :], 0.0))[-1], wait=[p1_done])
    prev_c = {}
    last_store = None
    for c in range(NC):
        pc = c % 2
        tld = []
        tld.append(E.emit("y", (lambda c=c, pc=pc: lambda y: y.dma_start(
            out=qf[:, pc, :, :],
            in_=q_d[:, c * 128:(c + 1) * 128].rearrange("(h p) t -> p h t", p=128)))(),
            dma=True,
            wait=[p1_done] + ([prev_c[("u", pc)]] if ("u", pc) in prev_c else [])))
        tld.append(E.emit("y", (lambda c=c, pc=pc: lambda y: y.dma_start(
            out=kf[:, pc, :, :],
            in_=k_d[:, c * 128:(c + 1) * 128].rearrange("(h p) t -> p h t", p=128)))(),
            dma=True))
        tld.append(E.emit("y", (lambda c=c, pc=pc: lambda y: y.dma_start(
            out=kt[:, pc, :], in_=k_d[:, c * 128:(c + 1) * 128], transpose=True))(),
            dma=True))
        tld.append(E.emit("y", (lambda c=c, pc=pc: lambda y: y.dma_start(
            out=vt[:, pc, :], in_=v_d[c * 128:(c + 1) * 128, :]))(), dma=True))
        tld.append(E.emit("y", (lambda c=c, pc=pc: lambda y: y.dma_start(
            out=gt[:, pc, :], in_=g_d[c * 128:(c + 1) * 128, :]))(), dma=True))
        txr = E.emit("y", (lambda c=c: lambda y: y.dma_start(
            out=xr[:, :, :],
            in_=t_in[:, c * 128:(c + 1) * 128].rearrange("(c2 p) t -> p c2 t", p=128)))(),
            dma=True, wait=[last_store] if last_store is not None else [p1_done])
        tldl = tld[-1]
        o_evs = []
        for h_ in range(H):
            bcol = beta[:, c, h_:h_ + 1]
            bg0 = pb[2]
            tg0 = E.emit("t", (lambda pc=pc, h_=h_: lambda t: t.matmul(
                bg0[:, 0:128], kf[:, pc, h_, :], kf[:, pc, h_, :],
                start=True, stop=True))(),
                wait=[tld[1]] + ([prev_c.get("g0u")] if prev_c.get("g0u") else []))
            tnn = E.emit("v", (lambda bcol=bcol: lambda v: v.scalar_tensor_tensor(
                nn_[:, :], bg0[:, 0:128], bcol, mls[:, :],
                op0=ALU.mult, op1=ALU.mult))(), wait=[tg0, tz],
                force_wait=[prev_c.get("nnu")] if prev_c.get("nnu") else [])
            btr = pb[3]
            ttr0 = E.emit("t", (lambda: lambda t: t.transpose(
                btr.bitcast(BF16)[:, 0:128], nn_[:, :], cb.idb[:]))(),
                wait=[tnn, cb.ready] + ([prev_c.get("ntu")] if prev_c.get("ntu") else []))
            tnt = E.emit("v", (lambda: lambda v: v.tensor_copy(
                nnt[:, :], btr.bitcast(BF16)[:, 0:128]))(), wait=[ttr0])
            tr1 = E.emit("v", (lambda: lambda v: v.tensor_tensor(
                rr_[:, :], cb.idb[:], nn_[:, :], op=ALU.subtract))(),
                force_wait=[tnt])
            tr2 = E.emit("v", (lambda: lambda v: v.tensor_tensor(
                rrt[:, :], cb.idb[:], nnt[:, :], op=ALU.subtract))(),
                force_wait=[tr1])
            tpw = E.emit("v", (lambda: lambda v: (v.tensor_copy(pw[:, :], nn_[:, :]),
                                                  v.tensor_copy(pwt[:, :], nnt[:, :]))[-1])(),
                         force_wait=[tr2])
            for j in range(6):
                bq = pb[4]
                tq1 = E.emit("t", (lambda: lambda t: t.matmul(
                    bq[:, 0:128], pwt[:, :], pw[:, :], start=True, stop=True))(),
                    wait=[tpw] + ([prev_c.get("bq4")] if prev_c.get("bq4") else []))
                tq2 = E.emit("t", (lambda: lambda t: t.matmul(
                    bq[:, 128:256], pw[:, :], pwt[:, :], start=True, stop=True))())
                tq3 = E.emit("v", (lambda: lambda v: (
                    v.tensor_copy(pw[:, :], bq.bitcast(BF16)[:, 0:128] if False
                                  else bq[:, 0:128]),
                    v.tensor_copy(pwt[:, :], bq[:, 128:256]))[-1])(), wait=[tq2],
                    force_wait=[tpw])
                prev_c["bq4"] = tq3
                bq2 = pb[5]
                tq4 = E.emit("t", (lambda: lambda t: t.matmul(
                    bq2[:, 0:128], pwt[:, :], rr_[:, :], start=True, stop=True))(),
                    wait=[tq3] + ([prev_c.get("bq5")] if prev_c.get("bq5") else []))
                tq5 = E.emit("t", (lambda: lambda t: t.matmul(
                    bq2[:, 128:256], rrt[:, :], pwt[:, :], start=True, stop=True))())                     if False else None
                tq5 = E.emit("t", (lambda: lambda t: t.matmul(
                    bq2[:, 128:256], rr_[:, :], pwt[:, :], start=True, stop=True))())
                tq6 = E.emit("v", (lambda: lambda v: (
                    v.tensor_tensor(rr_[:, :], rr_[:, :], bq2[:, 0:128], op=ALU.add),
                    v.tensor_tensor(rrt[:, :], rrt[:, :], bq2[:, 128:256], op=ALU.add))[-1])(),
                    wait=[tq5], force_wait=[tq3])
                prev_c["bq5"] = tq6
                tpw = tq6
            if dbg is not None and c == 0 and h_ == 0:
                E.emit("y", lambda y: y.dma_start(out=dbg["nn"], in_=nn_[:, :]),
                       dma=True, wait=[tpw])
                E.emit("y", lambda y: y.dma_start(out=dbg["rr"], in_=rr_[:, :]),
                       dma=True, wait=[tpw])
                E.emit("y", lambda y: y.dma_start(out=dbg["rrt"], in_=rrt[:, :]),
                       dma=True, wait=[tpw])
            bm0 = pb[0]
            tm0 = E.emit("t", (lambda pc=pc, h_=h_: lambda t: t.matmul(
                bm0[:, 0:256], kf[:, pc, h_, :], ssbb[:, h_, :],
                start=True, stop=True))(),
                wait=[tld[1], tz] + ([prev_c.get("bm0u")] if prev_c.get("bm0u") else []))
            tw0 = E.emit("v", (lambda pc=pc, h_=h_: lambda v: v.tensor_tensor(
                tmpo[:, :], vt[:, pc, h_ * 256:(h_ + 1) * 256], bm0[:, 0:256],
                op=ALU.subtract))(), wait=[tm0, tldl])
            tw1 = E.emit("v", (lambda bcol=bcol: lambda v: v.tensor_scalar(
                w0b[:, :], tmpo[:, :], bcol, None, op0=ALU.mult))(),
                force_wait=[tw0])
            prev_c["bm0u"] = tw1
            bu = pb[1]
            tu1 = E.emit("t", (lambda: lambda t: t.matmul(
                bu[:, 0:256], rrt[:, :], w0b[:, :], start=True, stop=True))(),
                wait=[tw1, tq6] + ([prev_c.get("bu1u")] if prev_c.get("bu1u") else []))
            tu2 = E.emit("v", (lambda: lambda v: v.tensor_copy(ub[:, :],
                                                               bu[:, 0:256]))(),
                         wait=[tu1])
            if dbg is not None and c == 0 and h_ == 0:
                E.emit("y", lambda y: y.dma_start(out=dbg["ub"], in_=ub[:, :]),
                       dma=True, wait=[tu2])
                E.emit("y", lambda y: y.dma_start(out=dbg["w0"], in_=w0b[:, :]),
                       dma=True, wait=[tu2])
            prev_c["bu1u"] = tu2
            prev_c["nnu"] = tu2
            bgq = pb[2]
            tgq = E.emit("t", (lambda pc=pc, h_=h_: lambda t: t.matmul(
                bgq[:, 128:256], kf[:, pc, h_, :], qf[:, pc, h_, :],
                start=True, stop=True))(), wait=[tld[0], tnn])
            tgq2 = E.emit("v", (lambda: lambda v: v.tensor_tensor(
                gkq[:, :], bgq[:, 128:256], mlq[:, :], op=ALU.mult))(),
                wait=[tgq], force_wait=[tu2])
            prev_c["g0u"] = tgq2
            bo = pb[0]
            to1 = E.emit("t", (lambda pc=pc, h_=h_: lambda t: t.matmul(
                bo[:, 256:512], qf[:, pc, h_, :], ssbb[:, h_, :],
                start=True, stop=False))(), wait=[tw1, tld[0]])
            to2 = E.emit("t", (lambda: lambda t: t.matmul(
                bo[:, 256:512], gkq[:, :], ub[:, :], start=False, stop=True))(),
                wait=[tgq2, tu2])
            bs2 = pb[3]
            ts_ = E.emit("t", (lambda pc=pc, h_=h_: lambda t: t.matmul(
                bs2[:, 256:512], kt[:, pc, h_ * 128:(h_ + 1) * 128], ub[:, :],
                start=True, stop=True))(), wait=[tu2, tld[2], tnt])
            tsu = E.emit("v", (lambda h_=h_: lambda v: v.tensor_tensor(
                ssb[:, h_, :], ssb[:, h_, :], bs2[:, 256:512], op=ALU.add))(),
                wait=[ts_, tz])
            tsb = E.emit("v", (lambda h_=h_: lambda v: v.tensor_copy(
                ssbb[:, h_, :], ssb[:, h_, :]))(), force_wait=[tsu])
            if dbg is not None and c == 0 and h_ == 0:
                E.emit("y", lambda y: y.dma_start(out=dbg["gkq"], in_=gkq[:, :]),
                       dma=True, wait=[tgq2])
                E.emit("y", lambda y: y.dma_start(out=dbg["s0"], in_=ssb[:, 0, :]),
                       dma=True, wait=[tsb])
            prev_c["ntu"] = tsb
            tsq = E.emit("s", (lambda h_=h_: lambda s: s.activation(
                out=tmpo[:, :], in_=bo[:, 256:512], func=AF.Square,
                accum_out=rst[:, :]))(), wait=[to2] +
                ([o_evs[-1]] if o_evs else ([prev_c.get("oev")] if prev_c.get("oev") else [])))
            tr_ = E.emit("v", (lambda: lambda v: v.tensor_scalar(
                rst[:, :], rst[:, :], 1.0 / 256.0, 1e-5, op0=ALU.mult,
                op1=ALU.add))(), force_wait=[tsq])
            tr2_ = E.emit("s", lambda s: s.activation(out=rst[:, :], in_=rst[:, :],
                                                      func=AF.Ln), wait=[tr_])
            tr3_ = E.emit("s", lambda s: s.activation(out=rst[:, :], in_=rst[:, :],
                                                      func=AF.Exp, scale=-0.5),
                          force_wait=[tr2_])
            to3 = E.emit("v", (lambda h_=h_: lambda v: v.tensor_scalar(
                otok[:, h_, :], bo[:, 256:512], rst[:, :], None, op0=ALU.mult))(),
                wait=[tr3_], force_wait=[tr_])
            prev_c["bm0u"] = to3
            to4 = E.emit("v", (lambda pc=pc, h_=h_: lambda v: v.tensor_tensor(
                onb[:, h_ * 256:(h_ + 1) * 256], otok[:, h_, :],
                gt[:, pc, h_ * 256:(h_ + 1) * 256], op=ALU.mult))(),
                wait=[tldl], force_wait=[to3])
            o_evs.append(to4)
            prev_c["oev"] = to4
        if dbg is not None and c == 0:
            E.emit("y", lambda y: y.dma_start(out=dbg["otok"], in_=otok[:, :, :]),
                   dma=True, wait=[o_evs[-1]])
            E.emit("y", lambda y: y.dma_start(out=dbg["onb"], in_=onb[:, :]),
                   dma=True, wait=[o_evs[-1]])
        ttr = []
        for i in range(8):
            bt_ = pb[4] if i % 2 == 0 else pb[5]
            t1 = E.emit("t", (lambda i=i, bt_=bt_: lambda t: t.transpose(
                bt_.bitcast(BF16)[:, 0:128], onb[:, i * 128:(i + 1) * 128],
                cb.idb[:]))(),
                wait=[o_evs[-1], cb.ready] + ([ttr[i - 2]] if i >= 2 else
                                              ([prev_c.get("bq4"), prev_c.get("bq5")])))
            t2 = E.emit("v", (lambda i=i, bt_=bt_: lambda v: v.tensor_copy(
                ofm[:, i, :], bt_.bitcast(BF16)[:, 0:128]))(), wait=[t1])
            ttr.append(t2)
        prev_c["bq4"], prev_c["bq5"] = ttr[6], ttr[7]
        if dbg is not None and c == 0:
            E.emit("y", lambda y: y.dma_start(out=dbg["ofm"], in_=ofm[:, :, :]),
                   dma=True, wait=[ttr[-1]])
            E.emit("y", lambda y: y.dma_start(out=dbg["wonb"], in_=wonb[:, :, :]),
                   dma=True, wait=[ttr[-1]])
        ty_toks = []
        for mt in range(8):
            def mk_o(mt=mt):
                def f(t):
                    h = None
                    for ch in range(NDC):
                        h = t.matmul(pb[2][:, 0:128],
                                     wonb[:, ch, mt * 128:(mt + 1) * 128],
                                     ofm[:, ch, :], start=(ch == 0),
                                     stop=(ch == NDC - 1))
                    return h
                return f
            w = [ttr[-1], twon]
            if ty_toks:
                w.append(ty_toks[-1])
            else:
                w.append(prev_c.get("g0u"))
            tom = E.emit("t", mk_o(), wait=[x for x in w if x is not None])
            ty_toks.append(E.emit("v", (lambda mt=mt: lambda v: v.tensor_tensor(
                yo[:, mt, :], pb[2][:, 0:128], xr[:, mt, :], op=ALU.add))(),
                wait=[tom, txr]))
        prev_c["g0u"] = ty_toks[-1]
        tmk = E.emit("v", lambda v: v.tensor_copy(yo[:, 0, 0:1], yo[:, 0, 0:1]),
                     force_wait=ty_toks)
        last_store = E.emit("y", (lambda c=c: lambda y: y.dma_start(
            out=t_out[:, c * 128:(c + 1) * 128].rearrange("(c2 p) t -> p c2 t", p=128),
            in_=yo[:, :, :]))(), dma=True, wait=[tmk])
        prev_c[("u", pc)] = ty_toks[-1]
    return last_store


def prep_l1_attn_a_weights(inputs):
    w = {}
    w["wq"] = np.ascontiguousarray(inputs["L1_attn_a_Wq"])
    w["wk"] = np.ascontiguousarray(inputs["L1_attn_a_Wk"])
    w["wv"] = np.ascontiguousarray(inputs["L1_attn_a_Wv"])
    w["wb"] = np.ascontiguousarray(inputs["L1_attn_a_Wb"])
    w["wg"] = np.ascontiguousarray(inputs["L1_attn_a_Wg"])
    w["wo"] = np.ascontiguousarray(inputs["L1_attn_a_Wo"])
    w["normw_col"] = np.ascontiguousarray(
        np.tile(inputs["L1_attn_a_norm_w"], 4).reshape(8, 128).T)
    tt = np.arange(128)
    w["mtril_s"] = (tt[None, :] < tt[:, None]).astype(BF)
    w["mtril_i"] = (tt[:, None] <= tt[None, :]).astype(BF)
    return w


def build_l1_attn_a_program(debug=False):
    from contextlib import ExitStack
    nc = bass.Bass("TRN2", target_bir_lowering=False)
    t_in = nc.dram_tensor("x_fm", [D, T], F32, kind="ExternalInput").ap()
    t_out = nc.dram_tensor("y_fm", [D, T], F32, kind="ExternalOutput").ap()
    W = {}
    for k, shp in [("wq", [D, 512]), ("wk", [D, 512]), ("wv", [D, D]),
                   ("wb", [D, 4]), ("wg", [D, D]), ("wo", [D, D]),
                   ("normw_col", [128, 8])]:
        W[k] = nc.dram_tensor(k, shp, F32, kind="ExternalInput").ap()
    for k in ["mtril_s", "mtril_i"]:
        W[k] = nc.dram_tensor(k, [128, 128], BF16, kind="ExternalInput").ap()
    kind = "ExternalOutput" if debug else "Internal"
    q_d = nc.dram_tensor("qd_scr", [512, T], BF16, kind=kind).ap()
    k_d = nc.dram_tensor("kd_scr", [512, T], BF16, kind=kind).ap()
    v_d = nc.dram_tensor("vd_scr", [T, D], BF16, kind=kind).ap()
    g_d = nc.dram_tensor("gd_scr", [T, D], BF16, kind=kind).ap()
    caps = declare_constants(nc)
    dbg = None
    if debug:
        dbg = {}
        for k, shp, dt in [("nn", [128, 128], BF16), ("rr", [128, 128], BF16),
                           ("rrt", [128, 128], BF16), ("ub", [128, 256], BF16),
                           ("w0", [128, 256], BF16), ("gkq", [128, 128], BF16),
                           ("s0", [128, 256], F32), ("otok", [128, 4, 256], F32),
                           ("onb", [128, 1024], BF16), ("ofm", [128, 8, 128], BF16),
                           ("wonb", [128, 8, 1024], BF16)]:
            dbg[k] = nc.dram_tensor("dbg_" + k, shp, dt, kind="ExternalOutput").ap()
    stack = ExitStack()
    with stack:
        E = Emitter(nc)
        cb = ConstBufs(E, stack, caps)
        tlast = emit_l1_attn_a(E, stack, cb, t_in, t_out, W, (q_d, k_d, v_d, g_d),
                               dbg=dbg)
        E.emit("y", lambda y: y.nop(), wait=[tlast])
        E.make_sems()
        with nc.Block() as block:
            E.materialize(block)
    return nc



def prep_l1_ffn_weights(inputs):
    w = {}
    w["router"] = np.ascontiguousarray(inputs["L1_ffn_router_DE"])
    w["w_in_e"] = np.ascontiguousarray(inputs["L1_ffn_w_in_eDF"])
    w["w_sw_e"] = np.ascontiguousarray(inputs["L1_ffn_w_swiglu_eDF"])
    w["w_out_e"] = np.ascontiguousarray(inputs["L1_ffn_w_out_eFD"])
    w["wsh_in_T"] = np.ascontiguousarray(inputs["L1_ffn_w_in_shared_FD"].T)
    w["wsh_sw_T"] = np.ascontiguousarray(inputs["L1_ffn_w_swiglu_FD"].T)
    w["wsh_out_T"] = np.ascontiguousarray(inputs["L1_ffn_w_out_shared_DF"].T)
    return w


def build_l1_ffn_program(nsb_limit=None, debug=False):
    from contextlib import ExitStack
    nc = bass.Bass("TRN2", target_bir_lowering=False)
    t_in = nc.dram_tensor("x_fm", [D, T], F32, kind="ExternalInput").ap()
    t_out = nc.dram_tensor("y_fm", [D, T], F32, kind="ExternalOutput").ap()
    W = {}
    W["router"] = nc.dram_tensor("router", [D, 8], F32, kind="ExternalInput").ap()
    wie = nc.dram_tensor("w_in_e", [8, D, 2048], F32, kind="ExternalInput").ap()
    wse = nc.dram_tensor("w_sw_e", [8, D, 2048], F32, kind="ExternalInput").ap()
    woe = nc.dram_tensor("w_out_e", [8, 2048, D], F32, kind="ExternalInput").ap()
    W["w_in_e"] = [wie[i] for i in range(8)]
    W["w_sw_e"] = [wse[i] for i in range(8)]
    W["w_out_e"] = [woe[i] for i in range(8)]
    W["wsh_in_T"] = nc.dram_tensor("wsh_in_T", [D, 2048], F32, kind="ExternalInput").ap()
    W["wsh_sw_T"] = nc.dram_tensor("wsh_sw_T", [D, 2048], F32, kind="ExternalInput").ap()
    W["wsh_out_T"] = nc.dram_tensor("wsh_out_T", [2048, D], F32, kind="ExternalInput").ap()
    caps = declare_constants(nc)
    dbg = None
    if debug:
        dbg = {
            "xn": nc.dram_tensor("dbg_xn", [128, NDC, 512], BF16, kind="ExternalOutput").ap(),
            "gR": nc.dram_tensor("dbg_gR", [1, 8, 512], BF16, kind="ExternalOutput").ap(),
            "hb": nc.dram_tensor("dbg_hb", [128, 16, 512], BF16, kind="ExternalOutput").ap(),
            "yacc": nc.dram_tensor("dbg_yacc", [128, NDC, 512], F32, kind="ExternalOutput").ap(),
        }

    stack = ExitStack()
    with stack:
        E = Emitter(nc)
        cb = ConstBufs(E, stack, caps)
        tlast = emit_l1_ffn(E, stack, cb, t_in, t_out, W, nsb_limit=nsb_limit,
                            dbg=dbg)
        E.emit("y", lambda y: y.nop(), wait=[tlast])
        E.make_sems()
        with nc.Block() as block:
            E.materialize(block)
    return nc


def prep_l1_attn_b_weights(inputs):
    w = {}
    qkv = inputs["L1_attn_b_W_qkv"].reshape(16, 3, 64, 1024)
    w["wqT"] = np.ascontiguousarray(qkv[:, 0].transpose(2, 0, 1).reshape(1024, 1024))
    w["wkT"] = np.ascontiguousarray(qkv[:, 1].transpose(2, 0, 1).reshape(1024, 1024))
    w["wvT"] = np.ascontiguousarray(qkv[:, 2].transpose(2, 0, 1).reshape(1024, 1024))
    w["woT"] = np.ascontiguousarray(inputs["L1_attn_b_W_o"].T)
    w["ln_gamma"] = np.ascontiguousarray(inputs["L1_attn_b_ln_gamma"].reshape(1024, 1))
    w["ln_beta"] = np.ascontiguousarray(inputs["L1_attn_b_ln_beta"].reshape(1024, 1))
    qq = np.arange(128)[:, None]
    kk = np.arange(512)[None, :]
    w["masks_q"] = np.stack([
        np.where(kk <= qq + m * 128, 0.0, -1e30) for m in range(4)]).astype(BF)
    kk2 = np.arange(128)[:, None]
    qq2 = np.arange(512)[None, :]
    w["masks_k"] = np.stack([
        np.where(qq2 >= kk2 + m * 128, 0.0, -1e30) for m in range(4)]).astype(BF)
    return w


def build_l1_attn_b_program():
    from contextlib import ExitStack
    nc = bass.Bass("TRN2", target_bir_lowering=False)
    t_in = nc.dram_tensor("x_fm", [D, T], F32, kind="ExternalInput").ap()
    t_out = nc.dram_tensor("y_fm", [D, T], F32, kind="ExternalOutput").ap()
    W = {}
    for k, shp in [("wqT", [D, D]), ("wkT", [D, D]), ("wvT", [D, D]),
                   ("woT", [D, D]), ("ln_gamma", [D, 1]), ("ln_beta", [D, 1]),
                   ]:
        W[k] = nc.dram_tensor(k, shp, F32, kind="ExternalInput").ap()
    for k in ["masks_q", "masks_k"]:
        W[k] = nc.dram_tensor(k, [4, 128, 512], BF16, kind="ExternalInput").ap()
    xn_dram = nc.dram_tensor("xn_scratch", [D, T], BF16, kind="Internal").ap()
    caps = declare_constants(nc)
    stack = ExitStack()
    with stack:
        E = Emitter(nc)
        cb = ConstBufs(E, stack, caps)
        tlast = emit_l1_attn_b(E, stack, cb, t_in, t_out, W, xn_dram)
        E.emit("y", lambda y: y.nop(), wait=[tlast])
        E.make_sems()
        with nc.Block() as block:
            E.materialize(block)
    return nc



_SEQ = [("l0a", "build_l0_attn_a_program", "prep_l0_attn_a_weights"),
        ("l0f", "build_l0_ffn_program", "prep_l0_ffn_weights"),
        ("l0b", "build_l0_attn_b_program", "prep_l0_attn_b_weights"),
        ("l0f", None, None),
        ("l1a", "build_l1_attn_a_program", "prep_l1_attn_a_weights"),
        ("l1f", "build_l1_ffn_program", "prep_l1_ffn_weights"),
        ("l1b", "build_l1_attn_b_program", "prep_l1_attn_b_weights"),
        ("l1f", None, None)]


def kernel(**inputs):
    x = np.ascontiguousarray(np.asarray(inputs["L0_attn_a_x"], np.float32))
    consts = host_constants()
    cmap = {"c_" + k: v for k, v in consts.items()}
    xs = [np.ascontiguousarray(x[i].T) for i in range(8)]
    cache = {}
    g = globals()
    for name, bf, pf in _SEQ:
        if name not in cache:
            cache[name] = (g[bf](), g[pf](inputs))
        nc, W = cache[name]
        in_maps = []
        for i in range(8):
            im = dict(x_fm=xs[i])
            im.update(W)
            im.update(cmap)
            in_maps.append(im)
        res = run_bass_kernel_spmd(nc, in_maps, core_ids=list(range(8)))
        xs = [np.asarray(r["y_fm"]) for r in res.results]
    return np.stack([s.T for s in xs]).astype(np.float32)
